# version 07
# baseline (speedup 1.0000x reference)
import numpy as np
import ml_dtypes
from contextlib import ExitStack
import concourse.bass as bass
import concourse.mybir as mybir
from concourse.bass_utils import run_bass_kernel_spmd

BF = ml_dtypes.bfloat16
F32 = mybir.dt.float32
BF16 = mybir.dt.bfloat16
I32 = mybir.dt.int32
AF = mybir.ActivationFunctionType
ALU = mybir.AluOpType
AX = mybir.AxisListType

NCORES = 8
B, L, HS = 16, 2048, 1024
NH, DK, DV = 4, 256, 256
CONV, FIRS, FIRL = 4, 5, 64
GH, GIN = 1024, 1040
II = 2816
SEQ_PER_CORE = B // NCORES
TOK = SEQ_PER_CORE * L
T512 = 512
NT_SEQ = L // T512
CH = HS // 128
CHI = II // 128
DECAY = 1.0 - 1.0 / 3000.0
EPS_FLOOR = 0.08 * DECAY
QMAX = 127.0
GC = float(np.sqrt(2.0 / np.pi))
GA = 0.044715

ENGINES = ("sync", "tensor", "vector", "scalar", "gpsimd")


class Sem:
    def __init__(self, h):
        self.h = h
        self.n = 0

    def inc(self, k=1):
        self.n += k
        return (self.h, k)


class _EngProxy:
    _NONREAL = {"wait_ge", "nop", "engine_nop"}

    def __init__(self, em, eng):
        object.__setattr__(self, "_em", em)
        object.__setattr__(self, "_eng", eng)

    def __getattr__(self, attr):
        em, eng = self._em, self._eng

        def call(*a, inc=None, **k):
            meta = {"inc": list(inc) if inc else []}

            def run(real, _attr=attr, _a=a, _k=k, _meta=meta):
                ins = getattr(real, _attr)(*_a, **_k)
                for s, n in _meta["inc"]:
                    ins.then_inc(s, n)
                return ins
            em.qs[eng].append(run)
            if attr not in self._NONREAL:
                em.lastmeta[eng] = meta
            elif attr == "wait_ge":
                pass
            else:
                em.lastmeta[eng] = meta
        return call

    def wait(self, sem, val=None):
        v = sem.n if val is None else val
        self.__getattr__("wait_ge")(sem.h, v)

    def mark(self, sem):
        em, eng = self._em, self._eng
        lm = em.lastmeta.get(eng)
        if lm is not None and not lm["inc"]:
            lm["inc"].append(sem.inc())
        else:
            nm = "engine_nop" if eng in ("vector", "gpsimd") else "nop"
            self.__getattr__(nm)(inc=[sem.inc()])
        return sem.n


class Em:
    def __init__(self):
        self.qs = {e: [] for e in ENGINES}
        self.lastmeta = {}
        self.pe = _EngProxy(self, "tensor")
        self.dve = _EngProxy(self, "vector")
        self.act = _EngProxy(self, "scalar")
        self.sp = _EngProxy(self, "sync")
        self.gp = _EngProxy(self, "gpsimd")

    def flush(self, block):
        for eng in ENGINES:
            fns = self.qs[eng]
            if not fns:
                continue

            def body(real, _fns=fns):
                for f in _fns:
                    f(real)
            getattr(block, eng)(body)


def _host_constants():
    jj, ii = np.meshgrid(np.arange(128), np.arange(128), indexing="ij")
    return dict(
        c_ident_b=np.eye(128, dtype=BF),
        c_ident_f=np.eye(128, dtype=np.float32),
        c_mask_ups=(jj < ii).astype(np.float32),
        c_mask_los=(jj > ii).astype(np.float32),
        c_mask_upi=(jj <= ii).astype(np.float32),
        c_ones_row_b=np.ones((1, 128), BF),
        c_ones_col_b=np.ones((128, 1), BF),
        c_e3_row=np.array([[0.0, 0.0, 0.0, 1.0]], np.float32),
    )


PH = {"prep": 0, "ln": 1, "a1": 2, "a2": 3, "a3": 4, "ffna": 5, "all": 9}


def build(upto="all", dbg=False, n_seq=SEQ_PER_CORE, n_tiles=NT_SEQ,
          prep_lim=99, dbg_delta=False):
    nc = bass.Bass("TRN2", target_bir_lowering=False)
    lim = PH[upto]
    em = Em()
    pe, dve, act, sp, gp = em.pe, em.dve, em.act, em.sp, em.gp

    X = nc.dram_tensor("x", [TOK, HS], F32, kind="ExternalInput").ap()
    OUT = nc.dram_tensor("out", [TOK, HS], F32, kind="ExternalOutput").ap()
    W = {}
    for nm, shp in [("Wq", [HS, HS]), ("Wk", [HS, HS]), ("Wv", [HS, HS]),
                    ("Wb", [NH, HS]), ("Wo", [HS, HS]), ("W1", [GH, GIN]),
                    ("W2", [NH, GH]),
                    ("convq", [HS, CONV]), ("convk", [HS, CONV]),
                    ("convv", [HS, CONV]),
                    ("firs", [HS, FIRS]), ("firl", [HS, FIRL]),
                    ("b1", [GH]), ("b2", [NH]), ("cb", [NH]), ("lt", [NH]),
                    ("onw", [DV]), ("clipx", [1]), ("clipd", [1]),
                    ("wg", [II, HS]), ("wu", [II, HS]), ("wd", [HS, II])]:
        W[nm] = nc.dram_tensor(nm, shp, F32, kind="ExternalInput").ap()
    CONSTS = {}
    for nm, arr in _host_constants().items():
        dt = BF16 if arr.dtype == BF else F32
        CONSTS[nm] = nc.dram_tensor(nm, list(arr.shape), dt,
                                    kind="ExternalInput").ap()
    D = {}
    dbg_names = {"odT_d", "v_d", "fs_d", "fl_d", "attn_d", "hpre_d",
                 "stall_d", "wqT_bf", "wg_bf"}
    for nm, shp, dt in [
        ("wqT_bf", [HS, HS], BF16), ("wkT_bf", [HS, HS], BF16),
        ("wvT_bf", [HS, HS], BF16),
        ("tmp_bf", [II, HS], BF16),
        ("wg_bf", [II, HS], BF16), ("wu_bf", [II, HS], BF16),
        ("wd_bf", [HS, II], BF16),
        ("odT_d", [CH * 128, TOK], BF16),
        ("v_d", [CH * 128, TOK], BF16),
        ("fs_d", [CH * 128, TOK], BF16),
        ("fl_d", [CH * 128, TOK], BF16),
        ("attn_d", [TOK, HS], F32),
        ("hpre_d", [II, TOK], F32),
        ("stall_d", [16, NH * L], BF16),
        ("bounce1", [128, 1], F32), ("sc_d", [1, 8], F32),
        ("cc1_in", [1, 1], F32), ("cc1_out", [1, 1], F32),
        ("cc2_in", [1, 1], F32), ("cc2_out", [1, 1], F32),
    ]:
        kind = "ExternalOutput" if (dbg and nm in dbg_names) else "Internal"
        D[nm] = nc.dram_tensor(nm, shp, dt, kind=kind).ap()
    if dbg:
        DBG = {nm: nc.dram_tensor("dbg_" + nm, shp, dt,
                                  kind="ExternalOutput").ap()
               for nm, shp, dt in [("mr", [128, 96], F32),
                                   ("tile", [128, CH * T512], BF16),
                                   ("tile2", [128, CH * T512], BF16),
                                   ("tile3", [128, CH * T512], BF16),
                                   ("misc", [1, 64], F32)]}

    es = ExitStack()

    def mk(es_, n, s, d=F32):
        return es_.enter_context(nc.sbuf_tensor(n, s, d))[:]

    def mkp(n, s):
        return es.enter_context(nc.psum_tensor(n, s, F32))[:]

    def SEM(n):
        return Sem(es.enter_context(nc.semaphore(n)))

    s_ld = SEM("s_ld")
    s_gp = SEM("s_gp")
    s_pe = SEM("s_pe")
    s_dve = SEM("s_dve")
    s_act = SEM("s_act")
    s_cc = SEM("s_cc")
    s_st = SEM("s_st")
    s_bar = SEM("s_bar")

    def barrier():
        for eng in (pe, dve, act, sp, gp):
            eng.mark(s_bar)
        v = s_bar.n
        for eng in (pe, dve, act, sp, gp):
            eng.wait(s_bar, v)

    kIb = mk(es, "kIb", [128, 128], BF16)
    kIf = mk(es, "kIf", [128, 128], F32)
    kMups = mk(es, "kMups", [128, 128], F32)
    kMlos = mk(es, "kMlos", [128, 128], F32)
    kMupi = mk(es, "kMupi", [128, 128], F32)
    kOnesR = mk(es, "kOnesR", [1, 128], BF16)
    kOnesC = mk(es, "kOnesC", [128, 1], BF16)
    kE3 = mk(es, "kE3", [1, 4], F32)
    convw = mk(es, "convw", [128, 3 * CH * CONV], F32)
    firw = mk(es, "firw", [128, CH * (FIRS + FIRL)], F32)
    b1c = mk(es, "b1c", [128, CH], F32)
    onwc = mk(es, "onwc", [128, 2], F32)
    smallrows = mk(es, "smallrows", [1, 16], F32)
    M_sb = mk(es, "M_sb", [128, 32], F32)
    R_sb = mk(es, "R_sb", [128, 32], F32)
    xmax_sb = mk(es, "xmax_sb", [128, 1], F32)
    hmax_sb = mk(es, "hmax_sb", [128, 1], F32)
    eps_c = mk(es, "eps_c", [128, 4], F32)
    sx_c = mk(es, "sx_c", [128, 4], F32)
    sd_c = mk(es, "sd_c", [128, 2], F32)
    biasbc = mk(es, "biasbc", [128, 16], F32)
    rtb = mk(es, "rtb", [128, 4], F32)
    W1bT = mk(es, "W1bT", [16, GH], BF16)
    W2T = mk(es, "W2T", [128, CH * 4], BF16)
    WbT = mk(es, "WbT", [128, CH * 4], BF16)
    misc3 = mk(es, "misc3", [1, 64], F32)
    row128 = mk(es, "row128", [1, 128], F32)
    lncols = mk(es, "lncols", [128, 8], F32)
    PB = [mkp(f"pb{i}", [128, T512]) for i in range(8)]

    for nm, tgt in [("c_ident_b", kIb), ("c_ident_f", kIf),
                    ("c_mask_ups", kMups), ("c_mask_los", kMlos),
                    ("c_mask_upi", kMupi), ("c_ones_row_b", kOnesR),
                    ("c_ones_col_b", kOnesC), ("c_e3_row", kE3)]:
        sp.dma_start(out=tgt, in_=CONSTS[nm], inc=[s_ld.inc(16)])
    for x_i, nm in enumerate(["convq", "convk", "convv"]):
        for c in range(CH):
            sp.dma_start(out=convw[:, (x_i * CH + c) * CONV:
                                   (x_i * CH + c + 1) * CONV],
                         in_=W[nm][c * 128:(c + 1) * 128, :],
                         inc=[s_ld.inc(16)])
    for c in range(CH):
        sp.dma_start(out=firw[:, c * FIRS:(c + 1) * FIRS],
                     in_=W["firs"][c * 128:(c + 1) * 128, :],
                     inc=[s_ld.inc(16)])
        sp.dma_start(out=firw[:, CH * FIRS + c * FIRL:
                              CH * FIRS + (c + 1) * FIRL],
                     in_=W["firl"][c * 128:(c + 1) * 128, :],
                     inc=[s_ld.inc(16)])
        sp.dma_start(out=b1c[:, c:c + 1],
                     in_=W["b1"][c * 128:(c + 1) * 128].rearrange("(k o) -> k o", o=1),
                     inc=[s_ld.inc(16)])
    for c in range(2):
        sp.dma_start(out=onwc[:, c:c + 1],
                     in_=W["onw"][c * 128:(c + 1) * 128].rearrange("(k o) -> k o", o=1),
                     inc=[s_ld.inc(16)])
    for i, nm in enumerate(["b2", "cb", "lt"]):
        sp.dma_start(out=smallrows[:, 4 * i:4 * i + 4],
                     in_=W[nm].rearrange("(o k) -> o k", o=1), inc=[s_ld.inc(16)])
    sp.dma_start(out=smallrows[:, 12:13],
                 in_=W["clipx"].rearrange("(o k) -> o k", o=1), inc=[s_ld.inc(16)])
    sp.dma_start(out=smallrows[:, 13:14],
                 in_=W["clipd"].rearrange("(o k) -> o k", o=1), inc=[s_ld.inc(16)])

    es_wp = ExitStack()
    wp_f = mk(es_wp, "wp_f", [128, II], F32)
    wp_i = mk(es_wp, "wp_i", [128, II], I32)
    wp_b = mk(es_wp, "wp_b", [128, II], BF16)
    wp_b2 = mk(es_wp, "wp_b2", [128, HS], BF16)
    wp_r = mk(es_wp, "wp_r", [128, 4], F32)
    lnx = mk(es_wp, "lnx", [128, HS], F32)
    lnsq = mk(es_wp, "lnsq", [128, HS], F32)

    dve.wait(s_ld)
    dve.memset(eps_c[:, 0:1], 1e-5)
    dve.memset(eps_c[:, 1:2], 4e-6)
    dve.memset(eps_c[:, 2:3], 1e-19)
    dve.memset(eps_c[:, 3:4], 0.0)
    dve.memset(xmax_sb, 0.0)
    dve.memset(hmax_sb, 0.0)

    if prep_lim >= 2:
        for src, dst in [("Wq", "wqT_bf"), ("Wk", "wkT_bf"), ("Wv", "wvT_bf")]:
            gp.wait(s_ld)
            gp.dma_start(out=D["tmp_bf"][0:HS, :], in_=W[src], inc=[s_gp.inc(16)])
            sp.wait(s_gp)
            for kc in range(CH):
                sp.dma_start_transpose(wp_b2,
                                       D["tmp_bf"][0:HS, kc * 128:(kc + 1) * 128],
                                       inc=[s_ld.inc(16)])
                sp.wait(s_ld)
                sp.dma_start(out=D[dst][kc * 128:(kc + 1) * 128, :], in_=wp_b2,
                             inc=[s_ld.inc(16)])
                sp.wait(s_ld)

    if prep_lim >= 3:
        gp.wait(s_ld)
        gp.dma_start(out=D["tmp_bf"][0:HS, :], in_=W["Wo"], inc=[s_gp.inc(16)])
        n_wo_cast = s_gp.n
        es_at_res = ExitStack()
        WoT = es_at_res.enter_context(
            nc.sbuf_tensor("WoT", [128, CH * HS], BF16, side="right"))[:]
        W1aT = es_at_res.enter_context(
            nc.sbuf_tensor("W1aT", [128, CH * HS], BF16, side="right"))[:]
        sp.wait(s_gp, n_wo_cast)
        for kc in range(CH):
            sp.dma_start_transpose(WoT[:, kc * HS:(kc + 1) * HS],
                                   D["tmp_bf"][0:HS, kc * 128:(kc + 1) * 128],
                                   inc=[s_ld.inc(16)])
        sp.wait(s_ld)
        gp.wait(s_ld)
        gp.dma_start(out=D["tmp_bf"][0:GH, 0:HS], in_=W["W1"][:, 0:HS],
                     inc=[s_gp.inc(16)])
        sp.wait(s_gp)
        for kc in range(CH):
            sp.dma_start_transpose(W1aT[:, kc * HS:(kc + 1) * HS],
                                   D["tmp_bf"][0:GH, kc * 128:(kc + 1) * 128],
                                   inc=[s_ld.inc(16)])

    if prep_lim >= 4:
        for c in range(CH):
            sp.dma_start(out=wp_f[:, c * 16:(c + 1) * 16],
                         in_=W["W1"][c * 128:(c + 1) * 128, HS:GIN],
                         inc=[s_ld.inc(16)])
        sp.dma_start(out=wp_f[0:4, 256:256 + HS], in_=W["W2"], inc=[s_ld.inc(16)])
        sp.dma_start(out=wp_f[0:4, 256 + HS:256 + 2 * HS], in_=W["Wb"],
                     inc=[s_ld.inc(16)])
        dve.wait(s_ld)
        dve.tensor_copy(wp_b[:, 0:CH * 16], wp_f[:, 0:CH * 16])
        dve.tensor_copy(wp_b[0:4, 256:256 + 2 * HS], wp_f[0:4, 256:256 + 2 * HS])
        mdve = dve.mark(s_dve)
        pe.wait(s_dve, mdve)
        for half in range(2):
            for cc in range(4):
                c = half * 4 + cc
                pe.matmul(PB[0][0:16, cc * 128:(cc + 1) * 128],
                          wp_b[:, c * 16:(c + 1) * 16], kIb, start=True,
                          stop=True)
            pe.mark(s_pe)
            act.wait(s_pe)
            act.copy(W1bT[0:16, half * 512:(half + 1) * 512],
                            PB[0][0:16, 0:512])
            act.mark(s_act)
            pe.wait(s_act)
        for c in range(CH):
            pe.matmul(PB[1][:, c * 4:(c + 1) * 4],
                      wp_b[0:4, 256 + c * 128:256 + (c + 1) * 128],
                      kIb[0:4, 0:4], start=True, stop=True)
            pe.matmul(PB[2][:, c * 4:(c + 1) * 4],
                      wp_b[0:4, 256 + HS + c * 128:256 + HS + (c + 1) * 128],
                      kIb[0:4, 0:4], start=True, stop=True)
        mpe = pe.mark(s_pe)
        act.wait(s_pe, mpe)
        act.activation(W2T, PB[1][:, 0:CH * 4], AF.Copy, scale=0.5)
        act.copy(WbT, PB[2][:, 0:CH * 4])
        mact = act.mark(s_act)
        pe.wait(s_act, mact)

        act.wait(s_ld)
        act.activation(misc3[0:1, 0:4], smallrows[0:1, 8:12], AF.Exp)
        act.mark(s_act)
        dve.wait(s_act)
        dve.reciprocal(misc3[0:1, 4:8], misc3[0:1, 0:4])
        dve.tensor_scalar(misc3[0:1, 8:12], smallrows[0:1, 4:8], DECAY, None,
                          ALU.mult)
        dve.tensor_copy(wp_b[0:1, 0:4], misc3[0:1, 4:8])
        dve.tensor_copy(wp_b[0:1, 4:8], smallrows[0:1, 0:4])
        dve.tensor_copy(wp_b[0:1, 8:12], misc3[0:1, 8:12])
        dve.tensor_copy(wp_b[0:1, 12:16], kE3[0:1, 0:4])
    if prep_lim >= 5:
        for h in range(NH):
            dve.tensor_scalar(misc3[0:1, 16 + 4 * h:20 + 4 * h], kE3[0:1, 0:4],
                              misc3[0:1, 8 + h:9 + h], None, ALU.mult)
            dve.drain()
            dve.tensor_tensor(misc3[0:1, 16 + 4 * h:20 + 4 * h],
                              misc3[0:1, 16 + 4 * h:20 + 4 * h],
                              smallrows[0:1, 0:4], ALU.add)
            dve.drain()
        dve.tensor_copy(wp_b[0:1, 16:32], misc3[0:1, 16:32])
        dve.mark(s_dve)
        pe.wait(s_dve)
        pe.matmul(PB[1][:, 0:4], kOnesR, wp_b[0:1, 0:4], start=True, stop=True)
        for h in range(NH):
            pe.matmul(PB[0][:, h * 4:(h + 1) * 4], kOnesR,
                      wp_b[0:1, 16 + 4 * h:20 + 4 * h], start=True, stop=True)
        pe.mark(s_pe)
        act.wait(s_pe)
        act.copy(rtb, PB[1][:, 0:4])
        act.copy(biasbc, PB[0][:, 0:16])
        act.mark(s_act)
        pe.wait(s_act)

    if prep_lim >= 6:
        def quant_weight(src, dst, nrow, ncol):
            for ck in range(nrow // 128):
                sp.dma_start(out=wp_f[:, 0:ncol],
                             in_=src[ck * 128:(ck + 1) * 128, :],
                             inc=[s_ld.inc(16)])
                dve.wait(s_ld)
                dve.wait(s_act)
                dve.reduce_max(wp_r[:, 0:1], wp_f[:, 0:ncol], axis=AX.X,
                               apply_absolute_value=True)
                dve.drain()
                dve.tensor_scalar(wp_r[:, 1:2], wp_r[:, 0:1], 1.0 / QMAX, None,
                                  ALU.mult)
                dve.drain()
                dve.reciprocal(wp_r[:, 2:3], wp_r[:, 1:2])
                dve.drain()
                dve.mark(s_dve)
                act.wait(s_dve)
                act.activation(wp_i[:, 0:ncol], wp_f[:, 0:ncol], AF.Copy,
                               scale=wp_r[:, 2:3])
                act.drain()
                act.copy(wp_b[:, 0:ncol], wp_i[:, 0:ncol])
                act.mark(s_act)
                dve.wait(s_act)
                dve.tensor_scalar(wp_b[:, 0:ncol], wp_b[:, 0:ncol], QMAX,
                                  -QMAX - 1.0, ALU.min, ALU.max)
                dve.drain()
                dve.tensor_scalar(wp_b[:, 0:ncol], wp_b[:, 0:ncol], wp_r[:, 1:2],
                                  None, ALU.mult, inc=[s_dve.inc()])
                gp.wait(s_dve)
                gp.dma_start(out=dst[ck * 128:(ck + 1) * 128, :],
                             in_=wp_b[:, 0:ncol], inc=[s_gp.inc(16)])
                sp.wait(s_gp)

        quant_weight(W["wg"], D["wg_bf"], II, HS)
        quant_weight(W["wu"], D["wu_bf"], II, HS)
        quant_weight(W["wd"], D["wd_bf"], HS, II)

    if prep_lim >= 7:
        inv_hs = 1.0 / HS
        for t in range(TOK // 128):
            sp.dma_start(out=lnx, in_=X[t * 128:(t + 1) * 128, :],
                         inc=[s_ld.inc(16)])
            dve.wait(s_ld)
            dve.reduce_sum(lncols[:, 0:1], lnx, axis=AX.X)
            dve.drain()
            dve.mark(s_dve)
            act.wait(s_dve)
            act.activation(lnsq, lnx, AF.Square, accum_out=lncols[:, 1:2])
            act.mark(s_act)
            dve.wait(s_act)
            dve.tensor_scalar(lncols[:, 2:3], lncols[:, 0:1], inv_hs, None,
                              ALU.mult)
            dve.tensor_scalar(lncols[:, 3:4], lncols[:, 1:2], inv_hs, None,
                              ALU.mult)
            dve.tensor_scalar(lncols[:, 4:5], lncols[:, 2:3], -1.0, None, ALU.mult)
            dve.drain()
            dve.scalar_tensor_tensor(lncols[:, 5:6], lncols[:, 4:5],
                                     lncols[:, 2:3], lncols[:, 3:4],
                                     ALU.mult, ALU.add)
            dve.mark(s_dve)
            act.wait(s_dve)
            act.activation(lncols[:, 6:7], lncols[:, 5:6], AF.Ln,
                           bias=eps_c[:, 0:1])
            act.activation(R_sb[:, t:t + 1], lncols[:, 6:7], AF.Exp, scale=-0.5)
            act.mark(s_act)
            dve.wait(s_act)
            dve.tensor_copy(M_sb[:, t:t + 1], lncols[:, 2:3])
            dve.tensor_mul(lncols[:, 7:8], lncols[:, 2:3], R_sb[:, t:t + 1])
            dve.drain()
            dve.tensor_scalar(lncols[:, 7:8], lncols[:, 7:8], -1.0, None, ALU.mult)
            dve.mark(s_dve)
            act.wait(s_dve)
            act.activation(lnsq, lnx, AF.Identity, scale=R_sb[:, t:t + 1],
                           bias=lncols[:, 7:8])
            act.mark(s_act)
            dve.wait(s_act)
            dve.reduce_max(lncols[:, 0:1], lnsq, axis=AX.X,
                           apply_absolute_value=True)
            dve.drain()
            dve.tensor_tensor(xmax_sb, xmax_sb, lncols[:, 0:1], ALU.max)
            dve.mark(s_dve)
            sp.wait(s_dve)
        n_ln_done = s_dve.n

    if prep_lim >= 8:
        gp.wait(s_dve, n_ln_done)
        gp.dma_start(out=D["bounce1"], in_=xmax_sb, inc=[s_gp.inc(16)])
        gp.wait(s_gp)
        gp.dma_start(out=row128,
                     in_=D["bounce1"].rearrange("(a k) o -> a (k o)", a=1),
                     inc=[s_gp.inc(16)])
        dve.wait(s_gp)
        dve.reduce_max(misc3[0:1, 20:21], row128, axis=AX.X,
                       inc=[s_dve.inc()])
        gp.wait(s_dve)
        gp.dma_start(out=D["cc1_in"], in_=misc3[0:1, 20:21], inc=[s_gp.inc(16)])
        gp.wait(s_gp)
        gp.collective_compute("AllReduce", ALU.max,
                              replica_groups=[list(range(8))],
                              ins=[D["cc1_in"].opt()], outs=[D["cc1_out"].opt()],
                              inc=[s_cc.inc()])
        cc1_done = s_cc.n

    barrier()
    es_wp.close()

    if lim >= 2:
        es_at = ExitStack()
        xnT = mk(es_at, "xnT", [128, CH * T512], BF16)
        xnb_g = mk(es_at, "xnb_g", [128, HS], BF16)
        xt_f = mk(es_at, "xt_f", [128, HS], F32)
        WstrT = mk(es_at, "WstrT", [128, CH * HS], BF16)
        preb = mk(es_at, "preb", [128, CH * (3 + T512)], BF16)
        pretail = mk(es_at, "pretail", [128, 3 * CH * 3], BF16)
        qT_t = mk(es_at, "qT_t", [128, CH * T512], BF16)
        kT_t = mk(es_at, "kT_t", [128, CH * T512], BF16)
        vT_t = mk(es_at, "vT_t", [128, CH * T512], BF16)
        betaT = mk(es_at, "betaT", [128, 16], F32)
        betab = mk(es_at, "betab", [4, T512], BF16)
        th_f = mk(es_at, "th_f", [128, T512], F32)
        cdiag = mk(es_at, "cdiag", [128, 128 * CONV], BF16)
        d_S = mk(es_at, "d_S", [128, NH * 2 * 256], F32)
        d_Sb = mk(es_at, "d_Sb", [128, 2 * 256], BF16)
        d_U = mk(es_at, "d_U", [128, 128], BF16)
        d_L = mk(es_at, "d_L", [128, 128], BF16)
        d_P = mk(es_at, "d_P", [128, 128], BF16)
        d_Bup = mk(es_at, "d_Bup", [128, 128], BF16)
        d_Blo = mk(es_at, "d_Blo", [128, 128], BF16)
        d_att = mk(es_at, "d_att", [128, 128], BF16)
        d_mrq = mk(es_at, "d_mrq", [128, 128], F32)
        d_khT = mk(es_at, "d_khT", [128, 256], BF16)
        d_kbT = mk(es_at, "d_kbT", [128, 256], BF16)
        d_vbT = mk(es_at, "d_vbT", [128, 256], BF16)
        d_wTn = mk(es_at, "d_wTn", [128, 256], BF16)
        d_u = mk(es_at, "d_u", [128, 256], BF16)
        d_qh = mk(es_at, "d_qh", [128, 256], BF16)
        d_sq = mk(es_at, "d_sq", [128, 256], BF16)
        d_cols = mk(es_at, "d_cols", [128, 8], F32)
        d_colsb = mk(es_at, "d_colsb", [128, 2], BF16)
        d_rq = mk(es_at, "d_rq", [1, 128], F32)
        d_rqb = mk(es_at, "d_rqb", [1, 128], BF16)
        d_rT = mk(es_at, "d_rT", [1, 256], BF16)
        d_odb = mk(es_at, "d_odb", [128, 256], BF16)
        diagb = mk(es_at, "diagb", [128, 128 * FIRL], BF16)
        diagsb = mk(es_at, "diagsb", [128, 128 * FIRS], BF16)
        vband = mk(es_at, "vband", [128, 63 + L], BF16)
        fir_st = mk(es_at, "fir_st", [128, 2 * T512], BF16)
        od_st = mk(es_at, "od_st", [128, T512], BF16)
        st_sq = mk(es_at, "st_sq", [128, T512], BF16)
        st_ab = mk(es_at, "st_ab", [128, T512], BF16)
        st_gin = mk(es_at, "st_gin", [128, 256], F32)
        st_acc = mk(es_at, "st_acc", [128, 192], F32)
        st_ginb = mk(es_at, "st_ginb", [128, 256], BF16)
        st_scr = mk(es_at, "st_scr", [128, 64], F32)
        st_all = mk(es_at, "st_all", [16, T512], BF16)
        st_al2 = mk(es_at, "st_al2", [16, T512], BF16)
        h1sh = mk(es_at, "h1sh", [128, CH * T512], BF16)
        h1b = mk(es_at, "h1b", [128, CH * T512], BF16)
        g_sc1 = mk(es_at, "g_sc1", [128, T512], F32)
        g_sc2 = mk(es_at, "g_sc2", [128, T512], F32)
        g_l = mk(es_at, "g_l", [128, 32], F32)
        wgt_b = mk(es_at, "wgt_b", [128, 16], BF16)
        wgtT = mk(es_at, "wgtT", [1, 4 * T512], BF16)
        br_fs = mk(es_at, "br_fs", [128, 2 * T512], BF16)
        br_fl = mk(es_at, "br_fl", [128, 2 * T512], BF16)
        br_od = mk(es_at, "br_od", [128, 2 * T512], BF16)
        br_v = mk(es_at, "br_v", [128, 2 * T512], BF16)
        oc_f = mk(es_at, "oc_f", [128, 2 * T512], F32)
        oc_b = mk(es_at, "oc_b", [128, 2 * T512], BF16)
        on_sb = mk(es_at, "on_sb", [128, CH * T512], BF16)
        rms_c = mk(es_at, "rms_c", [128, 8], F32)
        rms_b = mk(es_at, "rms_b", [128, 8], BF16)
        ao_f = mk(es_at, "ao_f", [128, HS], F32)

        def emit_xnT(tile512):
            for g in range(4):
                t = tile512 * 4 + g
                sp.wait(s_act)
                sp.dma_start(out=xt_f, in_=X[t * 128:(t + 1) * 128, :],
                             inc=[s_ld.inc(16)])
                dve.wait(s_ld)
                dve.tensor_mul(lncols[:, 0:1], M_sb[:, t:t + 1], R_sb[:, t:t + 1])
                dve.drain()
                dve.tensor_scalar(lncols[:, 0:1], lncols[:, 0:1], -1.0, None,
                                  ALU.mult)
                dve.mark(s_dve)
                act.wait(s_dve)
                act.activation(xnb_g, xt_f, AF.Identity, scale=R_sb[:, t:t + 1],
                               bias=lncols[:, 0:1], inc=[s_act.inc()])
                for half in range(2):
                    bank = PB[6 + half]
                    pe.wait(s_act)
                    pe.wait(s_dve)
                    for cc in range(4):
                        c = half * 4 + cc
                        pe.matmul(bank[:, cc * 128:(cc + 1) * 128],
                                  xnb_g[:, c * 128:(c + 1) * 128], kIb,
                                  start=True, stop=True)
                    pe.mark(s_pe)
                    dve.wait(s_pe)
                    for cc in range(4):
                        c = half * 4 + cc
                        dve.tensor_copy(
                            xnT[:, c * T512 + g * 128:c * T512 + (g + 1) * 128],
                            bank[:, cc * 128:(cc + 1) * 128])
                    dve.mark(s_dve)

        for s in range(n_seq):
            dve.wait(s_pe)
            dve.wait(s_act)
            dve.memset(pretail, 0.0)
            dve.memset(d_S, 0.0)
            dve.mark(s_dve)

            for tt in range(n_tiles):
                tile512 = s * NT_SEQ + tt
                tok0 = tile512 * T512
                emit_xnT(tile512)
                n_xnT = s_dve.n

                pe.wait(s_dve, n_xnT)
                for c in range(CH):
                    pe.matmul(PB[5][0:4, 0:T512], WbT[:, c * 4:(c + 1) * 4],
                              xnT[:, c * T512:(c + 1) * T512],
                              start=(c == 0), stop=(c == CH - 1))
                pe.mark(s_pe)
                act.wait(s_pe)
                act.activation(th_f[0:4, 0:T512], PB[5][0:4, 0:T512], AF.Tanh,
                               scale=0.5, inc=[s_act.inc()])
                dve.wait(s_act)
                dve.tensor_scalar(th_f[0:4, 0:T512], th_f[0:4, 0:T512], 0.5, 0.5,
                                  ALU.mult, ALU.add)
                dve.drain()
                dve.tensor_copy(betab, th_f[0:4, 0:T512])
                dve.mark(s_dve)
                pe.wait(s_dve)
                for sub in range(4):
                    pe.matmul(PB[5][:, 128 + sub * 4:128 + (sub + 1) * 4],
                              betab[0:4, sub * 128:(sub + 1) * 128],
                              kIb[0:4, 0:4], start=True, stop=True)
                pe.mark(s_pe)
                act.wait(s_pe)
                act.copy(betaT, PB[5][:, 128:144])
                act.mark(s_act)
                pe.wait(s_act)

                dve.wait(s_gp)
                for xi, outdst in enumerate([qT_t, kT_t, vT_t]):
                    wsrc = [D["wqT_bf"], D["wkT_bf"], D["wvT_bf"]][xi]
                    sp.wait(s_pe)
                    sp.dma_start(out=WstrT[:, 0:CH * HS].rearrange(
                                     "p (c o) -> p c o", o=HS),
                                 in_=wsrc.rearrange("(c p) o -> p c o", p=128),
                                 inc=[s_ld.inc(16)])
                    n_w = s_ld.n
                    dve.wait(s_act)
                    for c in range(CH):
                        dve.tensor_copy(
                            preb[:, c * (3 + T512):c * (3 + T512) + 3],
                            pretail[:, (xi * CH + c) * 3:(xi * CH + c + 1) * 3])
                    dve.mark(s_dve)
                    pe.wait(s_ld, n_w)
                    pe.wait(s_dve)
                    pe.wait(s_act)
                    proj_base = s_act.n
                    for ic in range(CH):
                        bank = PB[ic % 2]
                        pe.wait(s_act, max(proj_base, s_act.n - 1))
                        for kc in range(CH):
                            pe.matmul(bank[:, 0:T512],
                                      WstrT[:, kc * HS + ic * 128:
                                            kc * HS + (ic + 1) * 128],
                                      xnT[:, kc * T512:(kc + 1) * T512],
                                      start=(kc == 0), stop=(kc == CH - 1))
                        pe.mark(s_pe)
                        act.wait(s_pe)
                        act.copy(
                            preb[:, ic * (3 + T512) + 3:(ic + 1) * (3 + T512)],
                            bank[:, 0:T512])
                        act.mark(s_act)
                    n_pre = s_act.n
                    dve.wait(s_act, n_pre)
                    for ic in range(CH):
                        for j in range(CONV):
                            dve.tensor_scalar(
                                cdiag[:, j * 128:(j + 1) * 128], kIb,
                                convw[:, (xi * CH + ic) * CONV + j:
                                      (xi * CH + ic) * CONV + j + 1],
                                None, ALU.mult)
                        dve.tensor_copy(
                            pretail[:, (xi * CH + ic) * 3:(xi * CH + ic + 1) * 3],
                            preb[:, ic * (3 + T512) + T512:
                                 ic * (3 + T512) + T512 + 3])
                        dve.mark(s_dve)
                        pe.wait(s_dve)
                        bank = PB[2 + ic % 2]
                        pe.wait(s_dve)
                        for j in range(CONV):
                            pe.matmul(bank[:, 0:T512],
                                      cdiag[:, j * 128:(j + 1) * 128],
                                      preb[:, ic * (3 + T512) + j:
                                           ic * (3 + T512) + j + T512],
                                      start=(j == 0), stop=(j == CONV - 1))
                        pe.mark(s_pe)
                        act.wait(s_pe)
                        act.activation(th_f, bank[:, 0:T512], AF.Tanh, scale=0.5,
                                       inc=[s_act.inc()])
                        dve.wait(s_act)
                        if xi < 2:
                            dve.scalar_tensor_tensor(
                                outdst[:, ic * T512:(ic + 1) * T512],
                                th_f, 1.0, bank[:, 0:T512], ALU.add, ALU.mult,
                                inc=[s_dve.inc()])
                        else:
                            dve.scalar_tensor_tensor(
                                g_sc1, th_f, 1.0, bank[:, 0:T512],
                                ALU.add, ALU.mult)
                            dve.tensor_scalar(
                                outdst[:, ic * T512:(ic + 1) * T512],
                                g_sc1, 0.5, None, ALU.mult, inc=[s_dve.inc()])
                        pe.wait(s_dve)
                    if xi == 2:
                        gp.wait(s_dve)
                        for c in range(CH):
                            gp.dma_start(
                                out=D["v_d"][c * 128:(c + 1) * 128,
                                             tok0:tok0 + T512],
                                in_=vT_t[:, c * T512:(c + 1) * T512],
                                inc=[s_gp.inc(16)])

                dprev = {"d3": s_dve.n, "a2": s_act.n, "d7": s_dve.n}
                for ci in range(4):
                    for h in range(NH):
                        co = ci * 128
                        qF0 = qT_t[:, (2 * h) * T512 + co:(2 * h) * T512 + co + 128]
                        qF1 = qT_t[:, (2 * h + 1) * T512 + co:
                                   (2 * h + 1) * T512 + co + 128]
                        kF0 = kT_t[:, (2 * h) * T512 + co:(2 * h) * T512 + co + 128]
                        kF1 = kT_t[:, (2 * h + 1) * T512 + co:
                                   (2 * h + 1) * T512 + co + 128]
                        vF0 = vT_t[:, (2 * h) * T512 + co:(2 * h) * T512 + co + 128]
                        vF1 = vT_t[:, (2 * h + 1) * T512 + co:
                                   (2 * h + 1) * T512 + co + 128]
                        bcol = betaT[:, ci * 4 + h:ci * 4 + h + 1]
                        Sreg = d_S[:, h * 512:(h + 1) * 512]

                        pe.wait(s_dve, dprev["d3"])
                        pe.wait(s_act, dprev["a2"])
                        pe.matmul(PB[0][:, 0:128], kF0, kIb, start=True, stop=True)
                        pe.matmul(PB[0][:, 128:256], kF1, kIb, start=True,
                                  stop=True)
                        pe.matmul(PB[0][:, 256:384], vF0, kIb, start=True,
                                  stop=True)
                        pe.matmul(PB[0][:, 384:512], vF1, kIb, start=True,
                                  stop=True)
                        pe.mark(s_pe)
                        act.wait(s_pe)
                        act.wait(s_dve, dprev["d7"])
                        act.activation(d_sq, PB[0][:, 0:256], AF.Square,
                                       accum_out=d_cols[:, 0:1])
                        act.activation(d_cols[:, 1:2], d_cols[:, 0:1], AF.Ln,
                                       bias=eps_c[:, 1:2])
                        act.activation(d_cols[:, 2:3], d_cols[:, 1:2], AF.Exp,
                                       scale=-0.5)
                        act.activation(d_sq[:, 0:128], qF0, AF.Square)
                        act.activation(d_sq[:, 128:256], qF1, AF.Square)
                        dprev["a2"] = act.mark(s_act)
                        dve.wait(s_act)
                        dve.tensor_mul(d_cols[:, 3:4], d_cols[:, 2:3], bcol)
                        dve.drain()
                        dve.tensor_scalar(d_cols[:, 4:5], d_cols[:, 3:4], -1.0,
                                          None, ALU.mult)
                        dve.drain()
                        dve.tensor_copy(d_colsb[:, 0:1], d_cols[:, 2:3])
                        dve.tensor_copy(d_colsb[:, 1:2], d_cols[:, 4:5])
                        dve.tensor_scalar(d_kbT, PB[0][:, 0:256], d_cols[:, 3:4],
                                          None, ALU.mult)
                        dve.tensor_scalar(d_khT, PB[0][:, 0:256], d_cols[:, 2:3],
                                          None, ALU.mult)
                        dve.tensor_scalar(d_vbT, PB[0][:, 256:512], bcol,
                                          None, ALU.mult)
                        dprev["d3"] = dve.mark(s_dve)
                        pe.wait(s_dve)
                        pe.matmul(PB[2][0:1, 256:384], d_colsb[:, 0:1], kIb,
                                  start=True, stop=True)
                        pe.matmul(PB[2][0:1, 384:512], d_colsb[:, 1:2], kIb,
                                  start=True, stop=True)
                        pe.matmul(PB[1][:, 0:128], kF0, kF0, start=True,
                                  stop=False)
                        pe.matmul(PB[1][:, 0:128], kF1, kF1, start=False,
                                  stop=True)
                        pe.matmul(PB[1][:, 128:256], kF0, qF0, start=True,
                                  stop=False)
                        pe.matmul(PB[1][:, 128:256], kF1, qF1, start=False,
                                  stop=True)
                        pe.matmul(PB[1][0:1, 384:512], kOnesC, d_sq[:, 0:128],
                                  start=True, stop=False)
                        pe.matmul(PB[1][0:1, 384:512], kOnesC, d_sq[:, 128:256],
                                  start=False, stop=True)
                        pe.mark(s_pe)
                        act.wait(s_pe)
                        act.activation(d_rq, PB[1][0:1, 384:512], AF.Ln,
                                       bias=eps_c[0:1, 1:2])
                        act.activation(d_rqb, d_rq, AF.Exp, scale=-0.5)
                        act.copy(d_rT[0:1, 0:128], PB[2][0:1, 256:384])
                        act.copy(d_rT[0:1, 128:256], PB[2][0:1, 384:512])
                        act.mark(s_act)
                        pe.wait(s_act)
                        pe.matmul(PB[2][:, 0:128], kOnesR, d_rT[0:1, 128:256],
                                  start=True, stop=True)
                        pe.matmul(PB[2][:, 128:256], kOnesR, d_rT[0:1, 0:128],
                                  start=True, stop=True)
                        pe.matmul(PB[2][:, 256:384], kOnesR, d_rqb,
                                  start=True, stop=True)
                        pe.mark(s_pe)
                        dve.wait(s_pe)
                        dve.tensor_tensor(d_Bup, PB[2][:, 0:128], kMups, ALU.mult)
                        dve.tensor_tensor(d_Blo, PB[2][:, 128:256], kMlos,
                                          ALU.mult)
                        dve.tensor_tensor(d_qh[:, 0:128], qF0, PB[2][:, 256:384],
                                          ALU.mult)
                        dve.tensor_tensor(d_qh[:, 128:256], qF1, PB[2][:, 256:384],
                                          ALU.mult)
                        dve.scalar_tensor_tensor(d_U, PB[1][:, 0:128],
                                                 d_cols[:, 2:3], d_Bup,
                                                 ALU.mult, ALU.mult)
                        dve.scalar_tensor_tensor(d_L, PB[1][:, 0:128],
                                                 d_cols[:, 4:5], d_Blo,
                                                 ALU.mult, ALU.mult)
                        dve.tensor_tensor(d_mrq, PB[2][:, 256:384], kMupi,
                                          ALU.mult)
                        dve.drain()
                        dve.scalar_tensor_tensor(d_att, PB[1][:, 128:256],
                                                 d_cols[:, 2:3], d_mrq,
                                                 ALU.mult, ALU.mult)
                        dve.drain()
                        dve.tensor_tensor(d_P, d_U, kIb, ALU.add)
                        dprev["d7"] = dve.mark(s_dve)
                        for kk in range(1, 7):
                            pe.wait(s_dve)
                            pe.wait(s_act)
                            pe.matmul(PB[3][:, 0:128], d_L, d_U, start=True,
                                      stop=True)
                            pe.matmul(PB[2][:, 0:128], d_U, d_L, start=True,
                                      stop=True)
                            pe.matmul(PB[4][:, 0:128], kIb, d_P, start=True,
                                      stop=False)
                            pe.mark(s_pe)
                            act.wait(s_pe)
                            act.copy(d_U, PB[3][:, 0:128])
                            act.mark(s_act)
                            dve.wait(s_pe)
                            dve.tensor_copy(d_L, PB[2][:, 0:128])
                            dve.mark(s_dve)
                            pe.wait(s_act)
                            pe.wait(s_dve)
                            pe.matmul(PB[4][:, 0:128], d_L, d_P, start=False,
                                      stop=True)
                            pe.mark(s_pe)
                            dve.wait(s_pe)
                            dve.tensor_copy(d_P, PB[4][:, 0:128])
                            dve.mark(s_dve)
                        pe.wait(s_dve)
                        pe.matmul(PB[4][:, 128:256], d_kbT[:, 0:128], d_P,
                                  start=True, stop=True)
                        pe.matmul(PB[4][:, 256:384], d_kbT[:, 128:256], d_P,
                                  start=True, stop=True)
                        pe.mark(s_pe)
                        act.wait(s_pe)
                        act.activation(d_wTn[:, 0:128], PB[4][:, 128:256],
                                       AF.Copy, scale=-1.0)
                        act.activation(d_wTn[:, 128:256], PB[4][:, 256:384],
                                       AF.Copy, scale=-1.0)
                        act.mark(s_act)
                        dve.tensor_copy(d_Sb, Sreg)
                        dve.mark(s_dve)
                        pe.wait(s_act)
                        pe.wait(s_dve)
                        pe.matmul(PB[5][:, 0:256], d_P, d_vbT, start=True,
                                  stop=False)
                        pe.matmul(PB[5][:, 0:256], d_wTn[:, 0:128],
                                  d_Sb[:, 0:256], start=False, stop=False)
                        pe.matmul(PB[5][:, 0:256], d_wTn[:, 128:256],
                                  d_Sb[:, 256:512], start=False, stop=True)
                        pe.mark(s_pe)
                        dve.wait(s_pe)
                        dve.tensor_copy(d_u, PB[5][:, 0:256])
                        dve.mark(s_dve)
                        pe.wait(s_dve)
                        pe.matmul(PB[6][:, 0:128], d_Sb[:, 0:128],
                                  d_qh[:, 0:128], start=True, stop=False)
                        pe.matmul(PB[6][:, 0:128], d_Sb[:, 256:384],
                                  d_qh[:, 128:256], start=False, stop=False)
                        pe.matmul(PB[6][:, 0:128], d_u[:, 0:128], d_att,
                                  start=False, stop=True)
                        pe.matmul(PB[6][:, 128:256], d_Sb[:, 128:256],
                                  d_qh[:, 0:128], start=True, stop=False)
                        pe.matmul(PB[6][:, 128:256], d_Sb[:, 384:512],
                                  d_qh[:, 128:256], start=False, stop=False)
                        pe.matmul(PB[6][:, 128:256], d_u[:, 128:256], d_att,
                                  start=False, stop=True)
                        pe.mark(s_pe)
                        act.wait(s_pe)
                        act.wait(s_gp)
                        act.copy(d_odb, PB[6][:, 0:256])
                        act.drain()
                        act.mark(s_act)
                        gp.wait(s_act)
                        gp.dma_start(out=D["odT_d"][2 * h * 128:(2 * h + 1) * 128,
                                                    tok0 + co:tok0 + co + 128],
                                     in_=d_odb[:, 0:128], inc=[s_gp.inc(16)])
                        gp.dma_start(
                            out=D["odT_d"][(2 * h + 1) * 128:(2 * h + 2) * 128,
                                           tok0 + co:tok0 + co + 128],
                            in_=d_odb[:, 128:256], inc=[s_gp.inc(16)])
                        pe.matmul(PB[7][:, 0:256], d_khT[:, 0:128], d_u,
                                  start=True, stop=True)
                        pe.matmul(PB[7][:, 256:512], d_khT[:, 128:256], d_u,
                                  start=True, stop=True)
                        pe.mark(s_pe)
                        dve.wait(s_pe)
                        dve.tensor_tensor(Sreg[:, 0:256], Sreg[:, 0:256],
                                          PB[7][:, 0:256], ALU.add)
                        dve.tensor_tensor(Sreg[:, 256:512], Sreg[:, 256:512],
                                          PB[7][:, 256:512], ALU.add)
                        dve.mark(s_dve)
                        pe.wait(s_dve)
                        if dbg_delta and s == 0 and tt == 0 and ci == 0 and h == 0:
                            gp.wait(s_dve)
                            for nm_, ap_, off_ in [
                                ("U", d_U, 0), ("L", d_L, 128), ("P", d_P, 256),
                                ("att", d_att, 384), ("kbT", d_kbT, 512),
                                ("vbT", d_vbT, 768), ("u", d_u, 1024),
                                ("qh", d_qh, 1280), ("khT", d_khT, 1536),
                                ("wTn", d_wTn, 1792), ("Sb", d_Sb, 2048),
                                ("odb", d_odb, 2560), ("rqb", d_rqb, 2816),
                            ]:
                                w_ = ap_.shape[-1] if hasattr(ap_, "shape") else 128
                                gp.dma_start(out=DBG["tile2"][0:ap_.shape[0],
                                                              off_:off_ + ap_.shape[1]],
                                    in_=ap_, inc=[s_gp.inc(16)])
                            gp.dma_start(out=DBG["tile3"][:, 0:16],
                                         in_=d_cols.bitcast(BF16), inc=[s_gp.inc(16)])
                            gp.dma_start(out=DBG["tile3"][:, 16:144], in_=d_Bup,
                                         inc=[s_gp.inc(16)])
                            gp.dma_start(out=DBG["tile3"][:, 144:272], in_=d_Blo,
                                         inc=[s_gp.inc(16)])
                            gp.wait(s_gp)
                            barrier()

            if lim >= 3:
                for h in range(NH):
                    stat_base_done = False
                    for c2loc in range(2):
                        c2 = 2 * h + c2loc
                        dve.wait(s_pe)
                        for j in range(FIRL):
                            dve.tensor_scalar(diagb[:, j * 128:(j + 1) * 128], kIb,
                                              firw[:, CH * FIRS + c2 * FIRL + j:
                                                   CH * FIRS + c2 * FIRL + j + 1],
                                              None, ALU.mult)
                        for j in range(FIRS):
                            dve.tensor_scalar(diagsb[:, j * 128:(j + 1) * 128], kIb,
                                              firw[:, c2 * FIRS + j:
                                                   c2 * FIRS + j + 1],
                                              None, ALU.mult)
                        dve.memset(vband[:, 0:63], 0.0)
                        a2_diag = dve.mark(s_dve)
                        fb = {0: s_act.n, 1: s_act.n}
                        sp.wait(s_dve)
                        sp.wait(s_pe)
                        sp.dma_start(out=vband[:, 63:63 + L],
                                     in_=D["v_d"][c2 * 128:(c2 + 1) * 128,
                                                  s * L:(s + 1) * L],
                                     inc=[s_ld.inc(16)])
                        n_vb = s_ld.n
                        for sl in range(NT_SEQ):
                            t0 = sl * T512
                            tok0 = s * L + t0
                            pe.wait(s_ld, n_vb)
                            pe.wait(s_dve, a2_diag)
                            pe.wait(s_act, fb[sl % 2])
                            fsb = PB[sl % 2]
                            flb = PB[2 + sl % 2]
                            for j in range(FIRS):
                                pe.matmul(fsb[:, 0:T512],
                                          diagsb[:, j * 128:(j + 1) * 128],
                                          vband[:, t0 + 59 + j:t0 + 59 + j + T512],
                                          start=(j == 0), stop=(j == FIRS - 1))
                            for j in range(FIRL):
                                pe.matmul(flb[:, 0:T512],
                                          diagb[:, j * 128:(j + 1) * 128],
                                          vband[:, t0 + j:t0 + j + T512],
                                          start=(j == 0), stop=(j == FIRL - 1))
                            pe.mark(s_pe)
                            act.wait(s_pe)
                            act.wait(s_gp)
                            act.copy(fir_st[:, 0:T512], fsb[:, 0:T512])
                            act.copy(fir_st[:, T512:2 * T512],
                                            flb[:, 0:T512])
                            act.drain()
                            fb[sl % 2] = act.mark(s_act)
                            gp.wait(s_act)
                            gp.dma_start(out=D["fs_d"][c2 * 128:(c2 + 1) * 128,
                                                       tok0:tok0 + T512],
                                         in_=fir_st[:, 0:T512], inc=[s_gp.inc(16)])
                            gp.dma_start(out=D["fl_d"][c2 * 128:(c2 + 1) * 128,
                                                       tok0:tok0 + T512],
                                         in_=fir_st[:, T512:2 * T512],
                                         inc=[s_gp.inc(16)])
                            sp.wait(s_pe)
                            sp.dma_start(out=od_st,
                                         in_=D["odT_d"][c2 * 128:(c2 + 1) * 128,
                                                        tok0:tok0 + T512],
                                         inc=[s_ld.inc(16)])
                            srcs = [fir_st[:, 0:T512], fir_st[:, T512:2 * T512],
                                    od_st, vband[:, 63 + t0:63 + t0 + T512]]
                            dve.wait(s_ld)
                            dve.wait(s_act)
                            for ti, srcap in enumerate(srcs):
                                act.activation(st_sq, srcap, AF.Square)
                                act.activation(st_ab, srcap, AF.Abs)
                                act.mark(s_act)
                                dve.wait(s_act)
                                dve.mark(s_dve)
                                pe.wait(s_dve)
                                for sub in range(4):
                                    base_col = (sl * 4 + sub) * 12 + ti * 3
                                    pe.matmul(PB[4][:, base_col:base_col + 1],
                                              srcap[:, sub * 128:(sub + 1) * 128],
                                              kOnesC, start=True, stop=True)
                                    pe.matmul(PB[4][:, base_col + 1:base_col + 2],
                                              st_sq[:, sub * 128:(sub + 1) * 128],
                                              kOnesC, start=True, stop=True)
                                    pe.matmul(PB[4][:, base_col + 2:base_col + 3],
                                              st_ab[:, sub * 128:(sub + 1) * 128],
                                              kOnesC, start=True, stop=True)
                                pe.mark(s_pe)
                                act.wait(s_pe)
                                dve.wait(s_pe)
                        dve.wait(s_pe)
                        if c2loc == 0:
                            dve.tensor_copy(st_acc, PB[4][:, 0:192])
                        else:
                            dve.tensor_tensor(st_acc, st_acc, PB[4][:, 0:192],
                                              ALU.add)
                        dve.drain()
                        dve.mark(s_dve)
                        pe.wait(s_dve)
                    n_stats = s_pe.n
                    inv_dv = 1.0 / DV
                    IN3 = st_acc.rearrange("p (n k) -> p n k", k=3)
                    OUT4 = st_gin.rearrange("p (n s) -> p n s", s=4)
                    scr_a = g_sc1[:, 0:64].rearrange("p (n s) -> p n s", s=1)
                    scr_b = g_sc2[:, 0:64].rearrange("p (n s) -> p n s", s=1)
                    dve.wait(s_pe, n_stats)
                    dve.tensor_scalar(OUT4[:, :, 0:1], IN3[:, :, 0:1], inv_dv, None,
                                      ALU.mult)
                    dve.tensor_scalar(scr_a, IN3[:, :, 1:2], inv_dv, None,
                                      ALU.mult)
                    dve.tensor_scalar(OUT4[:, :, 2:3], IN3[:, :, 2:3], inv_dv, None,
                                      ALU.mult)
                    dve.tensor_scalar(scr_b, OUT4[:, :, 0:1], -1.0, None,
                                      ALU.mult)
                    dve.drain()
                    dve.tensor_tensor(scr_b, scr_b, OUT4[:, :, 0:1], ALU.mult)
                    dve.drain()
                    dve.tensor_tensor(OUT4[:, :, 1:2], scr_b, scr_a, ALU.add)
                    dve.mark(s_dve)
                    act.wait(s_dve)
                    act.activation(scr_a, IN3[:, :, 1:2], AF.Ln, bias=eps_c[:, 2:3])
                    act.activation(OUT4[:, :, 3:4], scr_a, AF.Exp, scale=0.5)
                    act.mark(s_act)
                    dve.wait(s_act)
                    dve.drain()
                    dve.tensor_copy(st_ginb, st_gin)
                    dve.mark(s_dve)
                    for tg in range(16):
                        pe.wait(s_dve)
                        bank = PB[5]
                        pe.matmul(bank[0:16, (tg % 4) * 128:(tg % 4 + 1) * 128],
                                  st_ginb[:, tg * 16:(tg + 1) * 16], kIb,
                                  start=True, stop=True)
                        if tg % 4 == 3:
                            pe.mark(s_pe)
                            act.wait(s_pe)
                            act.wait(s_gp)
                            act.copy(st_all[0:16, 0:512], bank[0:16, 0:512])
                            act.drain()
                            act.mark(s_act)
                            gp.wait(s_act)
                            gp.dma_start(
                                out=D["stall_d"][0:16, h * L + (tg - 3) * 128:
                                                 h * L + (tg + 1) * 128],
                                in_=st_all[0:16, 0:512], inc=[s_gp.inc(16)])
                            pe.wait(s_act)

            if lim >= 4:
                for sl in range(NT_SEQ):
                    tile512 = s * NT_SEQ + sl
                    tok0 = tile512 * T512
                    emit_xnT(tile512)
                    n_xnT = s_dve.n
                    pe.wait(s_dve, n_xnT)
                    for oc in range(CH):
                        bank = PB[oc % 2]
                        pe.wait(s_act)
                        for kc in range(CH):
                            pe.matmul(bank[:, 0:T512],
                                      W1aT[:, kc * HS + oc * 128:
                                           kc * HS + (oc + 1) * 128],
                                      xnT[:, kc * T512:(kc + 1) * T512],
                                      start=(kc == 0), stop=(kc == CH - 1))
                        pe.mark(s_pe)
                        act.wait(s_pe)
                        act.activation(h1sh[:, oc * T512:(oc + 1) * T512],
                                       bank[:, 0:T512], AF.Identity,
                                       bias=b1c[:, oc:oc + 1])
                        act.mark(s_act)
                    n_h1sh = s_act.n

                    for h in range(NH):
                        sp.wait(s_pe)
                        sp.dma_start(out=st_al2[0:16, 0:T512],
                                     in_=D["stall_d"][0:16, h * L + sl * T512:
                                                      h * L + (sl + 1) * T512],
                                     inc=[s_ld.inc(16)])
                        n_st = s_ld.n
                        dve.wait(s_act, n_h1sh)
                        pe.wait(s_ld, n_st)
                        for oc in range(CH):
                            bank = PB[2 + oc % 2]
                            pe.wait(s_dve)
                            pe.matmul(bank[:, 0:T512],
                                      W1bT[:, oc * 128:(oc + 1) * 128],
                                      st_al2[0:16, 0:T512],
                                      start=True, stop=True)
                            pe.mark(s_pe)
                            dve.wait(s_pe)
                            dve.scalar_tensor_tensor(
                                g_sc1, h1sh[:, oc * T512:(oc + 1) * T512], 1.0,
                                bank[:, 0:T512], ALU.mult, ALU.add)
                            dve.mark(s_dve)
                            act.wait(s_dve)
                            act.activation(g_sc2, g_sc1, AF.Square)
                            act.mark(s_act)
                            dve.wait(s_act)
                            dve.tensor_scalar(g_sc2, g_sc2, GC * GA, GC,
                                              ALU.mult, ALU.add)
                            dve.drain()
                            dve.tensor_mul(g_sc2, g_sc1, g_sc2)
                            dve.mark(s_dve)
                            act.wait(s_dve)
                            act.activation(g_sc2, g_sc2, AF.Tanh)
                            act.mark(s_act)
                            dve.wait(s_act)
                            dve.scalar_tensor_tensor(
                                h1b[:, oc * T512:(oc + 1) * T512], g_sc2, 1.0,
                                g_sc1, ALU.add, ALU.mult, inc=[s_dve.inc()])
                        pe.wait(s_dve)
                        for sub in range(4):
                            for oc in range(CH):
                                pe.matmul(PB[4][:, sub * 4:(sub + 1) * 4],
                                          h1b[:, oc * T512 + sub * 128:
                                              oc * T512 + (sub + 1) * 128],
                                          W2T[:, oc * 4:(oc + 1) * 4],
                                          start=(oc == 0), stop=(oc == CH - 1))
                        pe.mark(s_pe)
                        dve.wait(s_pe)
                        dve.tensor_tensor(
                            g_l[:, 0:16].rearrange("p (s g) -> p s g", g=4),
                            PB[4][:, 0:16].rearrange("p (s g) -> p s g", g=4),
                            biasbc[:, 4 * h:4 * h + 4].rearrange(
                                "p (o g) -> p o g", o=1).broadcast_to([128, 4, 4]), ALU.add)
                        dve.drain()
                        dve.tensor_scalar(g_l[:, 0:16], g_l[:, 0:16],
                                          rtb[:, h:h + 1], None, ALU.mult)
                        dve.drain()
                        dve.reduce_max(g_l[:, 16:20],
                                       g_l[:, 0:16].rearrange("p (s g) -> p s g",
                                                              g=4), axis=AX.X)
                        dve.tensor_tensor(
                            g_l[:, 0:16].rearrange("p (s g) -> p s g", g=4),
                            g_l[:, 0:16].rearrange("p (s g) -> p s g", g=4),
                            g_l[:, 16:20].rearrange("p (s o) -> p s o", o=1).broadcast_to(
                                [128, 4, 4]), ALU.subtract)
                        dve.mark(s_dve)
                        act.wait(s_dve)
                        act.activation(g_l[:, 0:16], g_l[:, 0:16], AF.Exp)
                        act.mark(s_act)
                        dve.wait(s_act)
                        dve.reduce_sum(g_l[:, 20:24],
                                       g_l[:, 0:16].rearrange("p (s g) -> p s g",
                                                              g=4), axis=AX.X)
                        dve.drain()
                        dve.reciprocal(g_l[:, 24:28], g_l[:, 20:24])
                        dve.drain()
                        dve.tensor_tensor(
                            g_l[:, 0:16].rearrange("p (s g) -> p s g", g=4),
                            g_l[:, 0:16].rearrange("p (s g) -> p s g", g=4),
                            g_l[:, 24:28].rearrange("p (s o) -> p s o", o=1).broadcast_to(
                                [128, 4, 4]), ALU.mult)
                        dve.drain()
                        dve.tensor_scalar(wgt_b, g_l[:, 0:16],
                                          1.0 - 4.0 * EPS_FLOOR, EPS_FLOOR,
                                          ALU.mult, ALU.add)
                        dve.mark(s_dve)
                        pe.wait(s_dve)
                        pe.wait(s_act)
                        for g in range(4):
                            for sub in range(4):
                                pe.matmul(PB[5][0:1, sub * 128:(sub + 1) * 128],
                                          wgt_b[:, sub * 4 + g:sub * 4 + g + 1], kIb,
                                          start=True, stop=True)
                            pe.mark(s_pe)
                            act.wait(s_pe)
                            act.copy(wgtT[0:1, g * T512:(g + 1) * T512],
                                            PB[5][0:1, 0:T512])
                            act.mark(s_act)
                            pe.wait(s_act)
                        sp.wait(s_dve)
                        for c2loc in range(2):
                            c2 = 2 * h + c2loc
                            for dst, dsrc in [(br_fs, "fs_d"), (br_fl, "fl_d"),
                                              (br_od, "odT_d"), (br_v, "v_d")]:
                                sp.dma_start(
                                    out=dst[:, c2loc * T512:(c2loc + 1) * T512],
                                    in_=D[dsrc][c2 * 128:(c2 + 1) * 128,
                                                tok0:tok0 + T512],
                                    inc=[s_ld.inc(16)])
                        n_br = s_ld.n
                        pe.wait(s_act)
                        for g, br in enumerate([br_fs, br_fl, br_od, br_v]):
                            pe.wait(s_dve)
                            pe.matmul(PB[6][:, 0:T512], kOnesR,
                                      wgtT[0:1, g * T512:(g + 1) * T512],
                                      start=True, stop=True)
                            pe.mark(s_pe)
                            dve.wait(s_pe)
                            dve.wait(s_ld, n_br)
                            for c2loc in range(2):
                                seg = slice(c2loc * T512, (c2loc + 1) * T512)
                                if g == 0:
                                    dve.tensor_tensor(oc_f[:, seg], br[:, seg],
                                                      PB[6][:, 0:T512], ALU.mult)
                                else:
                                    dve.tensor_tensor(g_sc1, br[:, seg],
                                                      PB[6][:, 0:T512], ALU.mult)
                                    dve.drain()
                                    dve.tensor_tensor(oc_f[:, seg], oc_f[:, seg],
                                                      g_sc1, ALU.add)
                            dve.mark(s_dve)
                        act.wait(s_dve)
                        act.activation(oc_b[:, 0:T512], oc_f[:, 0:T512], AF.Square)
                        act.activation(oc_b[:, T512:2 * T512], oc_f[:, T512:2 * T512],
                                       AF.Square)
                        act.mark(s_act)
                        pe.wait(s_act)
                        for sub in range(4):
                            pe.matmul(PB[7][:, sub:sub + 1],
                                      oc_b[:, sub * 128:(sub + 1) * 128], kOnesC,
                                      start=True, stop=False)
                            pe.matmul(PB[7][:, sub:sub + 1],
                                      oc_b[:, T512 + sub * 128:
                                           T512 + (sub + 1) * 128], kOnesC,
                                      start=False, stop=True)
                        pe.mark(s_pe)
                        act.wait(s_pe)
                        act.activation(rms_c[:, 0:4], PB[7][:, 0:4], AF.Ln,
                                       scale=1.0 / DV, bias=eps_c[:, 0:1])
                        act.activation(rms_c[:, 0:4], rms_c[:, 0:4], AF.Exp,
                                       scale=-0.5)
                        act.mark(s_act)
                        dve.wait(s_act)
                        dve.tensor_copy(rms_b[:, 0:4], rms_c[:, 0:4])
                        dve.mark(s_dve)
                        pe.wait(s_dve)
                        for sub in range(4):
                            pe.matmul(PB[5][0:1, sub * 128:(sub + 1) * 128],
                                      rms_b[:, sub:sub + 1], kIb,
                                      start=True, stop=True)
                        pe.mark(s_pe)
                        act.wait(s_pe)
                        act.copy(wgtT[0:1, 0:T512], PB[5][0:1, 0:T512])
                        act.mark(s_act)
                        pe.wait(s_act)
                        pe.matmul(PB[6][:, 0:T512], kOnesR, wgtT[0:1, 0:T512],
                                  start=True, stop=True)
                        pe.mark(s_pe)
                        dve.wait(s_pe)
                        for c2loc in range(2):
                            dve.scalar_tensor_tensor(
                                on_sb[:, (2 * h + c2loc) * T512:
                                      (2 * h + c2loc + 1) * T512],
                                oc_f[:, c2loc * T512:(c2loc + 1) * T512],
                                onwc[:, c2loc:c2loc + 1], PB[6][:, 0:T512],
                                ALU.mult, ALU.mult)
                        dve.mark(s_dve)
                        pe.wait(s_dve)

                    for sub in range(4):
                        bank = PB[sub % 2]
                        act.wait(s_ld)
                        for nh2 in range(2):
                            pe.wait(s_act)
                            for c in range(CH):
                                pe.matmul(bank[:, 0:T512],
                                          on_sb[:, c * T512 + sub * 128:
                                                c * T512 + (sub + 1) * 128],
                                          WoT[:, c * HS + nh2 * T512:
                                              c * HS + (nh2 + 1) * T512],
                                          start=(c == 0), stop=(c == CH - 1))
                            pe.mark(s_pe)
                            act.wait(s_pe)
                            act.copy(ao_f[:, nh2 * T512:(nh2 + 1) * T512],
                                            bank[:, 0:T512])
                            act.drain()
                            act.mark(s_act)
                        sp.wait(s_act)
                        sp.dma_start(out=D["attn_d"][tok0 + sub * 128:
                                                     tok0 + (sub + 1) * 128, :],
                                     in_=ao_f, inc=[s_ld.inc(16)])

        if dbg and not dbg_delta:
            barrier()
            sp.dma_start(out=DBG["tile"], in_=qT_t, inc=[s_ld.inc(16)])
            sp.dma_start(out=DBG["tile2"], in_=kT_t, inc=[s_ld.inc(16)])
            sp.dma_start(out=DBG["tile3"], in_=vT_t, inc=[s_ld.inc(16)])
            sp.wait(s_ld, s_ld.n)
        barrier()
        es_at.close()
        es_at_res.close()

    if lim >= 5:
        es_f = ExitStack()
        kOnesRf = mk(es_f, "kOnesRf", [1, 128], F32)
        f_xt = mk(es_f, "f_xt", [128, HS], F32)
        f_i32 = mk(es_f, "f_i32", [128, HS], I32)
        f_xq = mk(es_f, "f_xq", [128, HS], BF16)
        f_xqT = mk(es_f, "f_xqT", [128, CH * T512], BF16)
        f_wgT = mk(es_f, "f_wgT", [128, CH * II], BF16)
        f_wuT = mk(es_f, "f_wuT", [128, CH * II], BF16)
        f_th = mk(es_f, "f_th", [128, 2 * T512], F32)
        f_t1 = mk(es_f, "f_t1", [128, 2 * T512], F32)
        f_up = mk(es_f, "f_up", [128, 2 * T512], F32)
        f_hp = mk(es_f, "f_hp", [128, 2 * T512], F32)
        f_cols = mk(es_f, "f_cols", [128, 4], F32)

        dve.memset(kOnesRf, 1.0)
        dve.mark(s_dve)

        gp.wait(s_cc, cc1_done)
        gp.dma_start(out=misc3[0:1, 0:1], in_=D["cc1_out"], inc=[s_gp.inc(16)])
        act.wait(s_gp)
        act.activation(misc3[0:1, 1:2], smallrows[0:1, 12:13], AF.Tanh, scale=0.5)
        act.mark(s_act)
        dve.wait(s_act)
        dve.tensor_scalar(misc3[0:1, 1:2], misc3[0:1, 1:2], 0.5, 0.5,
                          ALU.mult, ALU.add)
        dve.drain()
        dve.tensor_mul(misc3[0:1, 2:3], misc3[0:1, 1:2], misc3[0:1, 0:1])
        dve.drain()
        dve.tensor_scalar(misc3[0:1, 2:3], misc3[0:1, 2:3], 1.0 / QMAX, None,
                          ALU.mult)
        dve.drain()
        dve.reciprocal(misc3[0:1, 3:4], misc3[0:1, 2:3])
        dve.tensor_mul(misc3[0:1, 4:5], misc3[0:1, 2:3], misc3[0:1, 2:3])
        dve.drain()
        dve.tensor_scalar(misc3[0:1, 4:5], misc3[0:1, 4:5], 0.5, None,
                          ALU.mult)
        dve.drain()
        dve.tensor_scalar(misc3[0:1, 5:6], misc3[0:1, 2:3], 0.5, None,
                          ALU.mult)
        dve.mark(s_dve)
        gp.wait(s_dve)
        gp.dma_start(out=D["sc_d"][0:1, 0:3], in_=misc3[0:1, 3:6],
                     inc=[s_gp.inc(16)])
        sp.wait(s_gp)
        for i in range(3):
            sp.dma_start(out=sx_c[:, i:i + 1],
                         in_=D["sc_d"][0:1, i:i + 1].broadcast_to([128, 1]),
                         inc=[s_ld.inc(16)])
        act.wait(s_ld)
        act.mark(s_act)

        for kc in range(CH):
            sp.dma_start_transpose(f_wgT[:, kc * II:(kc + 1) * II],
                                   D["wg_bf"][:, kc * 128:(kc + 1) * 128],
                                   inc=[s_ld.inc(16)])
            sp.dma_start_transpose(f_wuT[:, kc * II:(kc + 1) * II],
                                   D["wu_bf"][:, kc * 128:(kc + 1) * 128],
                                   inc=[s_ld.inc(16)])
        n_wgu = s_ld.n

        hp_guard = [0, 0]
        for tile512 in range(TOK // T512):
            tok0 = tile512 * T512
            for g in range(4):
                t = tile512 * 4 + g
                sp.wait(s_dve)
                sp.dma_start(out=f_xt, in_=X[t * 128:(t + 1) * 128, :],
                             inc=[s_ld.inc(16)])
                dve.wait(s_ld)
                dve.wait(s_act)
                dve.tensor_tensor(f_cols[:, 0:1], R_sb[:, t:t + 1], sx_c[:, 0:1],
                                  ALU.mult)
                dve.drain()
                dve.tensor_mul(f_cols[:, 1:2], M_sb[:, t:t + 1], f_cols[:, 0:1])
                dve.drain()
                dve.tensor_scalar(f_cols[:, 1:2], f_cols[:, 1:2], -1.0, None,
                                  ALU.mult)
                dve.mark(s_dve)
                act.wait(s_dve)
                act.wait(s_pe)
                act.activation(f_i32, f_xt, AF.Identity, scale=f_cols[:, 0:1],
                               bias=f_cols[:, 1:2])
                act.drain()
                act.copy(f_xq, f_i32)
                act.mark(s_act)
                dve.wait(s_act)
                dve.tensor_scalar(f_xq, f_xq, QMAX, -QMAX - 1.0, ALU.min,
                                  ALU.max)
                dve.mark(s_dve)
                for half in range(2):
                    bank = PB[6 + half]
                    pe.wait(s_dve)
                    for cc in range(4):
                        c = half * 4 + cc
                        pe.matmul(bank[:, cc * 128:(cc + 1) * 128],
                                  f_xq[:, c * 128:(c + 1) * 128], kIb,
                                  start=True, stop=True)
                    pe.mark(s_pe)
                    dve.wait(s_pe)
                    for cc in range(4):
                        c = half * 4 + cc
                        dve.tensor_copy(
                            f_xqT[:, c * T512 + g * 128:c * T512 + (g + 1) * 128],
                            bank[:, cc * 128:(cc + 1) * 128])
                    dve.mark(s_dve)
            n_xqT = s_dve.n
            pe.wait(s_dve, n_xqT)
            pe.wait(s_ld, n_wgu)
            ffa_dve = s_dve.n
            ffa_act = s_act.n
            pe.wait(s_dve)
            pe.wait(s_act)
            for ic in range(CHI):
                gbank = PB[ic % 2]
                ubank = PB[2 + ic % 2]
                pe.wait(s_dve, max(ffa_dve, s_dve.n - 1))
                pe.wait(s_act, max(ffa_act, s_act.n - 1))
                for kc in range(CH):
                    pe.matmul(gbank[:, 0:T512],
                              f_wgT[:, kc * II + ic * 128:kc * II + (ic + 1) * 128],
                              f_xqT[:, kc * T512:(kc + 1) * T512],
                              start=(kc == 0), stop=(kc == CH - 1))
                pe.mark(s_pe)
                n_g = s_pe.n
                for kc in range(CH):
                    pe.matmul(ubank[:, 0:T512],
                              f_wuT[:, kc * II + ic * 128:kc * II + (ic + 1) * 128],
                              f_xqT[:, kc * T512:(kc + 1) * T512],
                              start=(kc == 0), stop=(kc == CH - 1))
                pe.mark(s_pe)
                seg = slice((ic % 2) * T512, (ic % 2 + 1) * T512)
                act.wait(s_pe, n_g)
                act.activation(f_th[:, seg], gbank[:, 0:T512], AF.Tanh,
                               scale=sx_c[:, 2:3])
                act.wait(s_pe)
                act.activation(f_up[:, seg], ubank[:, 0:T512], AF.Identity,
                               scale=sx_c[:, 1:2])
                act.mark(s_act)
                dve.wait(s_act)
                hp_half = f_hp[:, (ic % 2) * T512:(ic % 2 + 1) * T512]
                dve.wait(s_gp, hp_guard[ic % 2])
                dve.scalar_tensor_tensor(f_t1[:, seg], f_th[:, seg], 1.0,
                                         gbank[:, 0:T512],
                                         ALU.add, ALU.mult)
                dve.drain()
                dve.tensor_mul(hp_half, f_t1[:, seg], f_up[:, seg])
                dve.drain()
                dve.reduce_max(f_cols[:, 2:3], hp_half, axis=AX.X,
                               apply_absolute_value=True)
                dve.drain()
                dve.tensor_tensor(hmax_sb, hmax_sb, f_cols[:, 2:3], ALU.max)
                dve.mark(s_dve)
                gp.wait(s_dve)
                gp.dma_start(out=D["hpre_d"][ic * 128:(ic + 1) * 128,
                                             tok0:tok0 + T512],
                             in_=hp_half, inc=[s_gp.inc(16)])
                hp_guard[ic % 2] = s_gp.n
        n_ffna = s_gp.n

        gp.wait(s_dve)
        gp.dma_start(out=D["bounce1"], in_=hmax_sb, inc=[s_gp.inc(16)])
        gp.wait(s_gp)
        gp.dma_start(out=row128,
                     in_=D["bounce1"].rearrange("(a k) o -> a (k o)", a=1),
                     inc=[s_gp.inc(16)])
        dve.wait(s_gp)
        dve.reduce_max(misc3[0:1, 8:9], row128, axis=AX.X,
                       inc=[s_dve.inc()])
        gp.wait(s_dve)
        gp.dma_start(out=D["cc2_in"], in_=misc3[0:1, 8:9], inc=[s_gp.inc(16)])
        gp.wait(s_gp)
        gp.collective_compute("AllReduce", ALU.max,
                              replica_groups=[list(range(8))],
                              ins=[D["cc2_in"].opt()], outs=[D["cc2_out"].opt()],
                              inc=[s_cc.inc()])
        gp.wait(s_cc)
        gp.dma_start(out=misc3[0:1, 9:10], in_=D["cc2_out"], inc=[s_gp.inc(16)])
        act.wait(s_gp)
        act.activation(misc3[0:1, 10:11], smallrows[0:1, 13:14], AF.Tanh,
                       scale=0.5)
        act.mark(s_act)
        dve.wait(s_act)
        dve.tensor_scalar(misc3[0:1, 10:11], misc3[0:1, 10:11], 0.5, 0.5,
                          ALU.mult, ALU.add)
        dve.drain()
        dve.tensor_mul(misc3[0:1, 11:12], misc3[0:1, 10:11], misc3[0:1, 9:10])
        dve.drain()
        dve.tensor_scalar(misc3[0:1, 11:12], misc3[0:1, 11:12], 1.0 / QMAX, None,
                          ALU.mult)
        dve.drain()
        dve.reciprocal(misc3[0:1, 12:13], misc3[0:1, 11:12])
        dve.drain()
        dve.mark(s_dve)
        gp.wait(s_dve)
        gp.dma_start(out=D["sc_d"][0:1, 3:5], in_=misc3[0:1, 11:13],
                     inc=[s_gp.inc(16)])
        sp.wait(s_gp)
        sp.dma_start(out=sd_c[:, 0:1],
                     in_=D["sc_d"][0:1, 4:5].broadcast_to([128, 1]),
                     inc=[s_ld.inc(16)])
        sp.dma_start(out=sd_c[:, 1:2],
                     in_=D["sc_d"][0:1, 3:4].broadcast_to([128, 1]),
                     inc=[s_ld.inc(16)])
        act.wait(s_ld)
        act.mark(s_act)

        barrier()
        es_f.close()

    if lim >= 9:
        es_f2 = ExitStack()
        kOnesRf2 = mk(es_f2, "kOnesRf2", [1, 128], F32)
        f_wdT = mk(es_f2, "f_wdT", [128, CHI * HS], BF16)
        f_hf = mk(es_f2, "f_hf", [128, CHI * T512], F32)
        f_hi = mk(es_f2, "f_hi", [128, CHI * T512], I32)
        f_hb = mk(es_f2, "f_hb", [128, CHI * T512], BF16)
        f2_x = mk(es_f2, "f2_x", [128, HS], F32)
        f2_a = mk(es_f2, "f2_a", [128, HS], F32)
        f2_o = mk(es_f2, "f2_o", [128, HS], F32)

        for ic in range(CHI):
            sp.dma_start_transpose(f_wdT[:, ic * HS:(ic + 1) * HS],
                                   D["wd_bf"][:, ic * 128:(ic + 1) * 128],
                                   inc=[s_ld.inc(16)])
        n_wd = s_ld.n

        for tile512 in range(TOK // T512):
            tok0 = tile512 * T512
            sp.wait(s_pe)
            for ic in range(CHI):
                sp.dma_start(out=f_hf[:, ic * T512:(ic + 1) * T512],
                             in_=D["hpre_d"][ic * 128:(ic + 1) * 128,
                                             tok0:tok0 + T512],
                             inc=[s_ld.inc(16)])
            act.wait(s_ld)
            act.activation(f_hi, f_hf, AF.Identity, scale=sd_c[:, 0:1])
            act.drain()
            act.copy(f_hb, f_hi)
            act.mark(s_act)
            dve.wait(s_act)
            dve.tensor_scalar(f_hb, f_hb, QMAX, -QMAX - 1.0, ALU.min, ALU.max)
            dve.mark(s_dve)
            n_hq = s_dve.n
            pe.wait(s_dve, n_hq)
            pe.wait(s_ld, n_wd)
            ffb_act = s_act.n
            pe.wait(s_act)
            for ts in range(4):
                for nh2 in range(2):
                    bank = PB[(ts * 2 + nh2) % 2]
                    pe.wait(s_act, max(ffb_act, s_act.n - 1))
                    for ic in range(CHI):
                        pe.matmul(bank[:, 0:T512],
                                  f_hb[:, ic * T512 + ts * 128:
                                       ic * T512 + (ts + 1) * 128],
                                  f_wdT[:, ic * HS + nh2 * T512:
                                        ic * HS + (nh2 + 1) * T512],
                                  start=(ic == 0), stop=(ic == CHI - 1))
                    pe.mark(s_pe)
                    act.wait(s_pe)
                    act.wait(s_st, s_st.n)
                    act.activation(f2_o[:, nh2 * T512:(nh2 + 1) * T512],
                                   bank[:, 0:T512], AF.Identity,
                                   scale=sd_c[:, 1:2])
                    act.mark(s_act)
                t = tile512 * 4 + ts
                sp.wait(s_dve)
                sp.dma_start(out=f2_x, in_=X[t * 128:(t + 1) * 128, :],
                             inc=[s_ld.inc(16)])
                sp.dma_start(out=f2_a, in_=D["attn_d"][t * 128:(t + 1) * 128, :],
                             inc=[s_ld.inc(16)])
                dve.wait(s_ld)
                dve.wait(s_act)
                dve.tensor_add(f2_o, f2_o, f2_a)
                dve.drain()
                dve.tensor_add(f2_o, f2_o, f2_x)
                dve.mark(s_dve)
                sp.wait(s_dve)
                sp.dma_start(out=OUT[t * 128:(t + 1) * 128, :], in_=f2_o,
                             inc=[s_st.inc(16)])

        sp.wait(s_st, s_st.n)
        gp.wait(s_st, s_st.n)

    if dbg:
        barrier()
        sp.dma_start(out=DBG["mr"][:, 0:32], in_=M_sb, allow_slow_non_contiguous=True, inc=[s_ld.inc(16)])
        sp.dma_start(out=DBG["mr"][:, 32:64], in_=R_sb, allow_slow_non_contiguous=True, inc=[s_ld.inc(16)])
        sp.dma_start(out=DBG["mr"][:, 64:65], in_=xmax_sb, allow_slow_non_contiguous=True, inc=[s_ld.inc(16)])
        sp.dma_start(out=DBG["mr"][:, 65:66], in_=hmax_sb, allow_slow_non_contiguous=True, inc=[s_ld.inc(16)])
        sp.dma_start(out=DBG["misc"], in_=misc3, inc=[s_ld.inc(16)])
    sp.wait(s_ld, s_ld.n)
    sp.wait(s_gp, s_gp.n)
    sp.wait(s_st, s_st.n)
    sp.wait(s_cc, s_cc.n)

    with nc.Block() as block:
        em.flush(block)
    return nc, ()


def _shard_inputs(inputs):
    x = np.ascontiguousarray(
        np.asarray(inputs["L0_attn_hidden_states"], np.float32))
    consts = _host_constants()
    wmap = dict(
        Wq="L0_attn_Wq", Wk="L0_attn_Wk", Wv="L0_attn_Wv", Wb="L0_attn_Wb",
        Wo="L0_attn_Wo", W1="L0_attn_gate_W1", W2="L0_attn_gate_W2",
        convq="L0_attn_conv_q_w", convk="L0_attn_conv_k_w",
        convv="L0_attn_conv_v_w", b1="L0_attn_gate_b1", b2="L0_attn_gate_b2",
        cb="L0_attn_gate_copy_bias", lt="L0_attn_gate_log_temp",
        onw="L0_attn_o_norm_w", clipx="L0_ffn_clip_x",
        clipd="L0_ffn_clip_down",
        wg="L0_ffn_w_gate", wu="L0_ffn_w_up", wd="L0_ffn_w_down",
    )
    base = {k: np.ascontiguousarray(np.asarray(inputs[v], np.float32))
            for k, v in wmap.items()}
    base["firs"] = np.ascontiguousarray(
        np.asarray(inputs["L0_attn_fir_short_filt"],
                   np.float32).reshape(NH * DV, FIRS))
    base["firl"] = np.ascontiguousarray(
        np.asarray(inputs["L0_attn_fir_long_filt"],
                   np.float32).reshape(NH * DV, FIRL))
    base.update(consts)
    in_maps = []
    for c in range(NCORES):
        d = dict(base)
        d["x"] = np.ascontiguousarray(
            x[c * SEQ_PER_CORE:(c + 1) * SEQ_PER_CORE].reshape(TOK, HS))
        in_maps.append(d)
    return in_maps


def kernel(**inputs):
    nc, _ = build()
    in_maps = _shard_inputs(inputs)
    res = run_bass_kernel_spmd(nc, in_maps, core_ids=list(range(NCORES)))
    out = np.concatenate(
        [r["out"].reshape(SEQ_PER_CORE, L, HS) for r in res.results], axis=0)
    return out.astype(np.float32)


# version 08 (final)
# speedup vs baseline: 1.0069x; 1.0069x over previous
import numpy as np
import ml_dtypes
from contextlib import ExitStack
import concourse.bass as bass
import concourse.mybir as mybir
from concourse.bass_utils import run_bass_kernel_spmd

BF = ml_dtypes.bfloat16
F32 = mybir.dt.float32
BF16 = mybir.dt.bfloat16
I32 = mybir.dt.int32
AF = mybir.ActivationFunctionType
ALU = mybir.AluOpType
AX = mybir.AxisListType

NCORES = 8
B, L, HS = 16, 2048, 1024
NH, DK, DV = 4, 256, 256
CONV, FIRS, FIRL = 4, 5, 64
GH, GIN = 1024, 1040
II = 2816
SEQ_PER_CORE = B // NCORES
TOK = SEQ_PER_CORE * L
T512 = 512
NT_SEQ = L // T512
CH = HS // 128
CHI = II // 128
DECAY = 1.0 - 1.0 / 3000.0
EPS_FLOOR = 0.08 * DECAY
QMAX = 127.0
GC = float(np.sqrt(2.0 / np.pi))
GA = 0.044715

ENGINES = ("sync", "tensor", "vector", "scalar", "gpsimd")


class Sem:
    def __init__(self, h):
        self.h = h
        self.n = 0

    def inc(self, k=1):
        self.n += k
        return (self.h, k)


class _EngProxy:
    _NONREAL = {"wait_ge", "nop", "engine_nop"}

    def __init__(self, em, eng):
        object.__setattr__(self, "_em", em)
        object.__setattr__(self, "_eng", eng)

    def __getattr__(self, attr):
        em, eng = self._em, self._eng

        def call(*a, inc=None, **k):
            meta = {"inc": list(inc) if inc else []}

            def run(real, _attr=attr, _a=a, _k=k, _meta=meta):
                ins = getattr(real, _attr)(*_a, **_k)
                for s, n in _meta["inc"]:
                    ins.then_inc(s, n)
                return ins
            em.qs[eng].append(run)
            if attr not in self._NONREAL:
                em.lastmeta[eng] = meta
            elif attr == "wait_ge":
                pass
            else:
                em.lastmeta[eng] = meta
        return call

    def wait(self, sem, val=None):
        v = sem.n if val is None else val
        self.__getattr__("wait_ge")(sem.h, v)

    def mark(self, sem):
        em, eng = self._em, self._eng
        lm = em.lastmeta.get(eng)
        if lm is not None and not lm["inc"]:
            lm["inc"].append(sem.inc())
        else:
            nm = "engine_nop" if eng in ("vector", "gpsimd") else "nop"
            self.__getattr__(nm)(inc=[sem.inc()])
        return sem.n


class Em:
    def __init__(self):
        self.qs = {e: [] for e in ENGINES}
        self.lastmeta = {}
        self.pe = _EngProxy(self, "tensor")
        self.dve = _EngProxy(self, "vector")
        self.act = _EngProxy(self, "scalar")
        self.sp = _EngProxy(self, "sync")
        self.gp = _EngProxy(self, "gpsimd")

    def flush(self, block):
        for eng in ENGINES:
            fns = self.qs[eng]
            if not fns:
                continue

            def body(real, _fns=fns):
                for f in _fns:
                    f(real)
            getattr(block, eng)(body)


def _host_constants():
    jj, ii = np.meshgrid(np.arange(128), np.arange(128), indexing="ij")
    return dict(
        c_ident_b=np.eye(128, dtype=BF),
        c_ident_f=np.eye(128, dtype=np.float32),
        c_mask_ups=(jj < ii).astype(np.float32),
        c_mask_los=(jj > ii).astype(np.float32),
        c_mask_upi=(jj <= ii).astype(np.float32),
        c_ones_row_b=np.ones((1, 128), BF),
        c_ones_col_b=np.ones((128, 1), BF),
        c_e3_row=np.array([[0.0, 0.0, 0.0, 1.0]], np.float32),
    )


PH = {"prep": 0, "ln": 1, "a1": 2, "a2": 3, "a3": 4, "ffna": 5, "all": 9}


def build(upto="all", dbg=False, n_seq=SEQ_PER_CORE, n_tiles=NT_SEQ,
          prep_lim=99, dbg_delta=False):
    nc = bass.Bass("TRN2", target_bir_lowering=False)
    lim = PH[upto]
    em = Em()
    pe, dve, act, sp, gp = em.pe, em.dve, em.act, em.sp, em.gp

    X = nc.dram_tensor("x", [TOK, HS], F32, kind="ExternalInput").ap()
    OUT = nc.dram_tensor("out", [TOK, HS], F32, kind="ExternalOutput").ap()
    W = {}
    for nm, shp in [("Wq", [HS, HS]), ("Wk", [HS, HS]), ("Wv", [HS, HS]),
                    ("Wb", [NH, HS]), ("Wo", [HS, HS]), ("W1", [GH, GIN]),
                    ("W2", [NH, GH]),
                    ("convq", [HS, CONV]), ("convk", [HS, CONV]),
                    ("convv", [HS, CONV]),
                    ("firs", [HS, FIRS]), ("firl", [HS, FIRL]),
                    ("b1", [GH]), ("b2", [NH]), ("cb", [NH]), ("lt", [NH]),
                    ("onw", [DV]), ("clipx", [1]), ("clipd", [1]),
                    ("wg", [II, HS]), ("wu", [II, HS]), ("wd", [HS, II])]:
        W[nm] = nc.dram_tensor(nm, shp, F32, kind="ExternalInput").ap()
    CONSTS = {}
    for nm, arr in _host_constants().items():
        dt = BF16 if arr.dtype == BF else F32
        CONSTS[nm] = nc.dram_tensor(nm, list(arr.shape), dt,
                                    kind="ExternalInput").ap()
    D = {}
    dbg_names = {"odT_d", "v_d", "fs_d", "fl_d", "attn_d", "hpre_d",
                 "stall_d", "wqT_bf", "wg_bf"}
    for nm, shp, dt in [
        ("wqT_bf", [HS, HS], BF16), ("wkT_bf", [HS, HS], BF16),
        ("wvT_bf", [HS, HS], BF16),
        ("tmp_bf", [II, HS], BF16),
        ("wg_bf", [II, HS], BF16), ("wu_bf", [II, HS], BF16),
        ("wd_bf", [HS, II], BF16),
        ("odT_d", [CH * 128, TOK], BF16),
        ("v_d", [CH * 128, TOK], BF16),
        ("fs_d", [CH * 128, TOK], BF16),
        ("fl_d", [CH * 128, TOK], BF16),
        ("attn_d", [TOK, HS], F32),
        ("hpre_d", [II, TOK], F32),
        ("stall_d", [16, NH * L], BF16),
        ("bounce1", [128, 1], F32), ("sc_d", [1, 8], F32),
        ("cc1_in", [1, 1], F32), ("cc1_out", [1, 1], F32),
        ("cc2_in", [1, 1], F32), ("cc2_out", [1, 1], F32),
    ]:
        kind = "ExternalOutput" if (dbg and nm in dbg_names) else "Internal"
        D[nm] = nc.dram_tensor(nm, shp, dt, kind=kind).ap()
    if dbg:
        DBG = {nm: nc.dram_tensor("dbg_" + nm, shp, dt,
                                  kind="ExternalOutput").ap()
               for nm, shp, dt in [("mr", [128, 96], F32),
                                   ("tile", [128, CH * T512], BF16),
                                   ("tile2", [128, CH * T512], BF16),
                                   ("tile3", [128, CH * T512], BF16),
                                   ("misc", [1, 64], F32)]}

    es = ExitStack()

    def mk(es_, n, s, d=F32):
        return es_.enter_context(nc.sbuf_tensor(n, s, d))[:]

    def mkp(n, s):
        return es.enter_context(nc.psum_tensor(n, s, F32))[:]

    def SEM(n):
        return Sem(es.enter_context(nc.semaphore(n)))

    s_ld = SEM("s_ld")
    s_gp = SEM("s_gp")
    s_pe = SEM("s_pe")
    s_dve = SEM("s_dve")
    s_act = SEM("s_act")
    s_cc = SEM("s_cc")
    s_st = SEM("s_st")
    s_bar = SEM("s_bar")

    def barrier():
        for eng in (pe, dve, act, sp, gp):
            eng.mark(s_bar)
        v = s_bar.n
        for eng in (pe, dve, act, sp, gp):
            eng.wait(s_bar, v)

    kIb = mk(es, "kIb", [128, 128], BF16)
    kIf = mk(es, "kIf", [128, 128], F32)
    kMups = mk(es, "kMups", [128, 128], F32)
    kMlos = mk(es, "kMlos", [128, 128], F32)
    kMupi = mk(es, "kMupi", [128, 128], F32)
    kOnesR = mk(es, "kOnesR", [1, 128], BF16)
    kOnesC = mk(es, "kOnesC", [128, 1], BF16)
    kE3 = mk(es, "kE3", [1, 4], F32)
    convw = mk(es, "convw", [128, 3 * CH * CONV], F32)
    firw = mk(es, "firw", [128, CH * (FIRS + FIRL)], F32)
    b1c = mk(es, "b1c", [128, CH], F32)
    onwc = mk(es, "onwc", [128, 2], F32)
    smallrows = mk(es, "smallrows", [1, 16], F32)
    M_sb = mk(es, "M_sb", [128, 32], F32)
    R_sb = mk(es, "R_sb", [128, 32], F32)
    xmax_sb = mk(es, "xmax_sb", [128, 1], F32)
    hmax_sb = mk(es, "hmax_sb", [128, 1], F32)
    eps_c = mk(es, "eps_c", [128, 4], F32)
    sx_c = mk(es, "sx_c", [128, 4], F32)
    sd_c = mk(es, "sd_c", [128, 2], F32)
    biasbc = mk(es, "biasbc", [128, 16], F32)
    rtb = mk(es, "rtb", [128, 4], F32)
    W1bT = mk(es, "W1bT", [16, GH], BF16)
    W2T = mk(es, "W2T", [128, CH * 4], BF16)
    WbT = mk(es, "WbT", [128, CH * 4], BF16)
    misc3 = mk(es, "misc3", [1, 64], F32)
    row128 = mk(es, "row128", [1, 128], F32)
    lncols = mk(es, "lncols", [128, 8], F32)
    PB = [mkp(f"pb{i}", [128, T512]) for i in range(8)]

    for nm, tgt in [("c_ident_b", kIb), ("c_ident_f", kIf),
                    ("c_mask_ups", kMups), ("c_mask_los", kMlos),
                    ("c_mask_upi", kMupi), ("c_ones_row_b", kOnesR),
                    ("c_ones_col_b", kOnesC), ("c_e3_row", kE3)]:
        sp.dma_start(out=tgt, in_=CONSTS[nm], inc=[s_ld.inc(16)])
    for x_i, nm in enumerate(["convq", "convk", "convv"]):
        for c in range(CH):
            sp.dma_start(out=convw[:, (x_i * CH + c) * CONV:
                                   (x_i * CH + c + 1) * CONV],
                         in_=W[nm][c * 128:(c + 1) * 128, :],
                         inc=[s_ld.inc(16)])
    for c in range(CH):
        sp.dma_start(out=firw[:, c * FIRS:(c + 1) * FIRS],
                     in_=W["firs"][c * 128:(c + 1) * 128, :],
                     inc=[s_ld.inc(16)])
        sp.dma_start(out=firw[:, CH * FIRS + c * FIRL:
                              CH * FIRS + (c + 1) * FIRL],
                     in_=W["firl"][c * 128:(c + 1) * 128, :],
                     inc=[s_ld.inc(16)])
        sp.dma_start(out=b1c[:, c:c + 1],
                     in_=W["b1"][c * 128:(c + 1) * 128].rearrange("(k o) -> k o", o=1),
                     inc=[s_ld.inc(16)])
    for c in range(2):
        sp.dma_start(out=onwc[:, c:c + 1],
                     in_=W["onw"][c * 128:(c + 1) * 128].rearrange("(k o) -> k o", o=1),
                     inc=[s_ld.inc(16)])
    for i, nm in enumerate(["b2", "cb", "lt"]):
        sp.dma_start(out=smallrows[:, 4 * i:4 * i + 4],
                     in_=W[nm].rearrange("(o k) -> o k", o=1), inc=[s_ld.inc(16)])
    sp.dma_start(out=smallrows[:, 12:13],
                 in_=W["clipx"].rearrange("(o k) -> o k", o=1), inc=[s_ld.inc(16)])
    sp.dma_start(out=smallrows[:, 13:14],
                 in_=W["clipd"].rearrange("(o k) -> o k", o=1), inc=[s_ld.inc(16)])

    es_wp = ExitStack()
    wp_f = mk(es_wp, "wp_f", [128, II], F32)
    wp_i = mk(es_wp, "wp_i", [128, II], I32)
    wp_b = mk(es_wp, "wp_b", [128, II], BF16)
    wp_b2 = mk(es_wp, "wp_b2", [128, HS], BF16)
    wp_r = mk(es_wp, "wp_r", [128, 4], F32)
    lnx = mk(es_wp, "lnx", [128, HS], F32)
    lnsq = mk(es_wp, "lnsq", [128, HS], F32)

    dve.wait(s_ld)
    dve.memset(eps_c[:, 0:1], 1e-5)
    dve.memset(eps_c[:, 1:2], 4e-6)
    dve.memset(eps_c[:, 2:3], 1e-19)
    dve.memset(eps_c[:, 3:4], 0.0)
    dve.memset(xmax_sb, 0.0)
    dve.memset(hmax_sb, 0.0)

    if prep_lim >= 2:
        for src, dst in [("Wq", "wqT_bf"), ("Wk", "wkT_bf"), ("Wv", "wvT_bf")]:
            gp.wait(s_ld)
            gp.dma_start(out=D["tmp_bf"][0:HS, :], in_=W[src], inc=[s_gp.inc(16)])
            sp.wait(s_gp)
            for kc in range(CH):
                sp.dma_start_transpose(wp_b2,
                                       D["tmp_bf"][0:HS, kc * 128:(kc + 1) * 128],
                                       inc=[s_ld.inc(16)])
                sp.wait(s_ld)
                sp.dma_start(out=D[dst][kc * 128:(kc + 1) * 128, :], in_=wp_b2,
                             inc=[s_ld.inc(16)])
                sp.wait(s_ld)

    if prep_lim >= 3:
        gp.wait(s_ld)
        gp.dma_start(out=D["tmp_bf"][0:HS, :], in_=W["Wo"], inc=[s_gp.inc(16)])
        n_wo_cast = s_gp.n
        es_at_res = ExitStack()
        WoT = es_at_res.enter_context(
            nc.sbuf_tensor("WoT", [128, CH * HS], BF16, side="right"))[:]
        W1aT = es_at_res.enter_context(
            nc.sbuf_tensor("W1aT", [128, CH * HS], BF16, side="right"))[:]
        sp.wait(s_gp, n_wo_cast)
        for kc in range(CH):
            sp.dma_start_transpose(WoT[:, kc * HS:(kc + 1) * HS],
                                   D["tmp_bf"][0:HS, kc * 128:(kc + 1) * 128],
                                   inc=[s_ld.inc(16)])
        sp.wait(s_ld)
        gp.wait(s_ld)
        gp.dma_start(out=D["tmp_bf"][0:GH, 0:HS], in_=W["W1"][:, 0:HS],
                     inc=[s_gp.inc(16)])
        sp.wait(s_gp)
        for kc in range(CH):
            sp.dma_start_transpose(W1aT[:, kc * HS:(kc + 1) * HS],
                                   D["tmp_bf"][0:GH, kc * 128:(kc + 1) * 128],
                                   inc=[s_ld.inc(16)])

    if prep_lim >= 4:
        for c in range(CH):
            sp.dma_start(out=wp_f[:, c * 16:(c + 1) * 16],
                         in_=W["W1"][c * 128:(c + 1) * 128, HS:GIN],
                         inc=[s_ld.inc(16)])
        sp.dma_start(out=wp_f[0:4, 256:256 + HS], in_=W["W2"], inc=[s_ld.inc(16)])
        sp.dma_start(out=wp_f[0:4, 256 + HS:256 + 2 * HS], in_=W["Wb"],
                     inc=[s_ld.inc(16)])
        dve.wait(s_ld)
        dve.tensor_copy(wp_b[:, 0:CH * 16], wp_f[:, 0:CH * 16])
        dve.tensor_copy(wp_b[0:4, 256:256 + 2 * HS], wp_f[0:4, 256:256 + 2 * HS])
        mdve = dve.mark(s_dve)
        pe.wait(s_dve, mdve)
        for half in range(2):
            for cc in range(4):
                c = half * 4 + cc
                pe.matmul(PB[0][0:16, cc * 128:(cc + 1) * 128],
                          wp_b[:, c * 16:(c + 1) * 16], kIb, start=True,
                          stop=True)
            pe.mark(s_pe)
            act.wait(s_pe)
            act.copy(W1bT[0:16, half * 512:(half + 1) * 512],
                            PB[0][0:16, 0:512])
            act.mark(s_act)
            pe.wait(s_act)
        for c in range(CH):
            pe.matmul(PB[1][:, c * 4:(c + 1) * 4],
                      wp_b[0:4, 256 + c * 128:256 + (c + 1) * 128],
                      kIb[0:4, 0:4], start=True, stop=True)
            pe.matmul(PB[2][:, c * 4:(c + 1) * 4],
                      wp_b[0:4, 256 + HS + c * 128:256 + HS + (c + 1) * 128],
                      kIb[0:4, 0:4], start=True, stop=True)
        mpe = pe.mark(s_pe)
        act.wait(s_pe, mpe)
        act.activation(W2T, PB[1][:, 0:CH * 4], AF.Copy, scale=0.5)
        act.copy(WbT, PB[2][:, 0:CH * 4])
        mact = act.mark(s_act)
        pe.wait(s_act, mact)

        act.wait(s_ld)
        act.activation(misc3[0:1, 0:4], smallrows[0:1, 8:12], AF.Exp)
        act.mark(s_act)
        dve.wait(s_act)
        dve.reciprocal(misc3[0:1, 4:8], misc3[0:1, 0:4])
        dve.tensor_scalar(misc3[0:1, 8:12], smallrows[0:1, 4:8], DECAY, None,
                          ALU.mult)
        dve.tensor_copy(wp_b[0:1, 0:4], misc3[0:1, 4:8])
        dve.tensor_copy(wp_b[0:1, 4:8], smallrows[0:1, 0:4])
        dve.tensor_copy(wp_b[0:1, 8:12], misc3[0:1, 8:12])
        dve.tensor_copy(wp_b[0:1, 12:16], kE3[0:1, 0:4])
    if prep_lim >= 5:
        for h in range(NH):
            dve.tensor_scalar(misc3[0:1, 16 + 4 * h:20 + 4 * h], kE3[0:1, 0:4],
                              misc3[0:1, 8 + h:9 + h], None, ALU.mult)
            dve.drain()
            dve.tensor_tensor(misc3[0:1, 16 + 4 * h:20 + 4 * h],
                              misc3[0:1, 16 + 4 * h:20 + 4 * h],
                              smallrows[0:1, 0:4], ALU.add)
            dve.drain()
        dve.tensor_copy(wp_b[0:1, 16:32], misc3[0:1, 16:32])
        dve.mark(s_dve)
        pe.wait(s_dve)
        pe.matmul(PB[1][:, 0:4], kOnesR, wp_b[0:1, 0:4], start=True, stop=True)
        for h in range(NH):
            pe.matmul(PB[0][:, h * 4:(h + 1) * 4], kOnesR,
                      wp_b[0:1, 16 + 4 * h:20 + 4 * h], start=True, stop=True)
        pe.mark(s_pe)
        act.wait(s_pe)
        act.copy(rtb, PB[1][:, 0:4])
        act.copy(biasbc, PB[0][:, 0:16])
        act.mark(s_act)
        pe.wait(s_act)

    if prep_lim >= 6:
        def quant_weight(src, dst, nrow, ncol):
            for ck in range(nrow // 128):
                sp.dma_start(out=wp_f[:, 0:ncol],
                             in_=src[ck * 128:(ck + 1) * 128, :],
                             inc=[s_ld.inc(16)])
                dve.wait(s_ld)
                dve.wait(s_act)
                dve.reduce_max(wp_r[:, 0:1], wp_f[:, 0:ncol], axis=AX.X,
                               apply_absolute_value=True)
                dve.drain()
                dve.tensor_scalar(wp_r[:, 1:2], wp_r[:, 0:1], 1.0 / QMAX, None,
                                  ALU.mult)
                dve.drain()
                dve.reciprocal(wp_r[:, 2:3], wp_r[:, 1:2])
                dve.drain()
                dve.mark(s_dve)
                act.wait(s_dve)
                act.activation(wp_i[:, 0:ncol], wp_f[:, 0:ncol], AF.Copy,
                               scale=wp_r[:, 2:3])
                act.drain()
                act.copy(wp_b[:, 0:ncol], wp_i[:, 0:ncol])
                act.mark(s_act)
                dve.wait(s_act)
                dve.tensor_scalar(wp_b[:, 0:ncol], wp_b[:, 0:ncol], QMAX,
                                  -QMAX - 1.0, ALU.min, ALU.max)
                dve.drain()
                dve.tensor_scalar(wp_b[:, 0:ncol], wp_b[:, 0:ncol], wp_r[:, 1:2],
                                  None, ALU.mult, inc=[s_dve.inc()])
                gp.wait(s_dve)
                gp.dma_start(out=dst[ck * 128:(ck + 1) * 128, :],
                             in_=wp_b[:, 0:ncol], inc=[s_gp.inc(16)])
                sp.wait(s_gp)

        quant_weight(W["wg"], D["wg_bf"], II, HS)
        quant_weight(W["wu"], D["wu_bf"], II, HS)
        quant_weight(W["wd"], D["wd_bf"], HS, II)

    if prep_lim >= 7:
        inv_hs = 1.0 / HS
        for t in range(TOK // 128):
            sp.dma_start(out=lnx, in_=X[t * 128:(t + 1) * 128, :],
                         inc=[s_ld.inc(16)])
            dve.wait(s_ld)
            dve.reduce_sum(lncols[:, 0:1], lnx, axis=AX.X)
            dve.drain()
            dve.mark(s_dve)
            act.wait(s_dve)
            act.activation(lnsq, lnx, AF.Square, accum_out=lncols[:, 1:2])
            act.mark(s_act)
            dve.wait(s_act)
            dve.tensor_scalar(lncols[:, 2:3], lncols[:, 0:1], inv_hs, None,
                              ALU.mult)
            dve.tensor_scalar(lncols[:, 3:4], lncols[:, 1:2], inv_hs, None,
                              ALU.mult)
            dve.tensor_scalar(lncols[:, 4:5], lncols[:, 2:3], -1.0, None, ALU.mult)
            dve.drain()
            dve.scalar_tensor_tensor(lncols[:, 5:6], lncols[:, 4:5],
                                     lncols[:, 2:3], lncols[:, 3:4],
                                     ALU.mult, ALU.add)
            dve.mark(s_dve)
            act.wait(s_dve)
            act.activation(lncols[:, 6:7], lncols[:, 5:6], AF.Ln,
                           bias=eps_c[:, 0:1])
            act.activation(R_sb[:, t:t + 1], lncols[:, 6:7], AF.Exp, scale=-0.5)
            act.mark(s_act)
            dve.wait(s_act)
            dve.tensor_copy(M_sb[:, t:t + 1], lncols[:, 2:3])
            dve.tensor_mul(lncols[:, 7:8], lncols[:, 2:3], R_sb[:, t:t + 1])
            dve.drain()
            dve.tensor_scalar(lncols[:, 7:8], lncols[:, 7:8], -1.0, None, ALU.mult)
            dve.mark(s_dve)
            act.wait(s_dve)
            act.activation(lnsq, lnx, AF.Identity, scale=R_sb[:, t:t + 1],
                           bias=lncols[:, 7:8])
            act.mark(s_act)
            dve.wait(s_act)
            dve.reduce_max(lncols[:, 0:1], lnsq, axis=AX.X,
                           apply_absolute_value=True)
            dve.drain()
            dve.tensor_tensor(xmax_sb, xmax_sb, lncols[:, 0:1], ALU.max)
            dve.mark(s_dve)
            sp.wait(s_dve)
        n_ln_done = s_dve.n

    if prep_lim >= 8:
        gp.wait(s_dve, n_ln_done)
        gp.dma_start(out=D["bounce1"], in_=xmax_sb, inc=[s_gp.inc(16)])
        gp.wait(s_gp)
        gp.dma_start(out=row128,
                     in_=D["bounce1"].rearrange("(a k) o -> a (k o)", a=1),
                     inc=[s_gp.inc(16)])
        dve.wait(s_gp)
        dve.reduce_max(misc3[0:1, 20:21], row128, axis=AX.X,
                       inc=[s_dve.inc()])
        gp.wait(s_dve)
        gp.dma_start(out=D["cc1_in"], in_=misc3[0:1, 20:21], inc=[s_gp.inc(16)])
        gp.wait(s_gp)
        gp.collective_compute("AllReduce", ALU.max,
                              replica_groups=[list(range(8))],
                              ins=[D["cc1_in"].opt()], outs=[D["cc1_out"].opt()],
                              inc=[s_cc.inc()])
        cc1_done = s_cc.n

    barrier()
    es_wp.close()

    if lim >= 2:
        es_at = ExitStack()
        xnT = mk(es_at, "xnT", [128, CH * T512], BF16)
        xnb_g = mk(es_at, "xnb_g", [128, HS], BF16)
        xt_f = mk(es_at, "xt_f", [128, HS], F32)
        WstrT = mk(es_at, "WstrT", [128, CH * HS], BF16)
        preb = mk(es_at, "preb", [128, CH * (3 + T512)], BF16)
        pretail = mk(es_at, "pretail", [128, 3 * CH * 3], BF16)
        qT_t = mk(es_at, "qT_t", [128, CH * T512], BF16)
        kT_t = mk(es_at, "kT_t", [128, CH * T512], BF16)
        vT_t = mk(es_at, "vT_t", [128, CH * T512], BF16)
        betaT = mk(es_at, "betaT", [128, 16], F32)
        betab = mk(es_at, "betab", [4, T512], BF16)
        th_f = mk(es_at, "th_f", [128, T512], F32)
        cdiag = mk(es_at, "cdiag", [128, 128 * CONV], BF16)
        d_S = mk(es_at, "d_S", [128, NH * 2 * 256], F32)
        d_Sb = mk(es_at, "d_Sb", [128, 2 * 256], BF16)
        d_U = mk(es_at, "d_U", [128, 128], BF16)
        d_L = mk(es_at, "d_L", [128, 128], BF16)
        d_P = mk(es_at, "d_P", [128, 128], BF16)
        d_Bup = mk(es_at, "d_Bup", [128, 128], BF16)
        d_Blo = mk(es_at, "d_Blo", [128, 128], BF16)
        d_att = mk(es_at, "d_att", [128, 128], BF16)
        d_mrq = mk(es_at, "d_mrq", [128, 128], F32)
        d_khT = mk(es_at, "d_khT", [128, 256], BF16)
        d_kbT = mk(es_at, "d_kbT", [128, 256], BF16)
        d_vbT = mk(es_at, "d_vbT", [128, 256], BF16)
        d_wTn = mk(es_at, "d_wTn", [128, 256], BF16)
        d_u = mk(es_at, "d_u", [128, 256], BF16)
        d_qh = mk(es_at, "d_qh", [128, 256], BF16)
        d_sq = mk(es_at, "d_sq", [128, 256], BF16)
        d_cols = mk(es_at, "d_cols", [128, 8], F32)
        d_colsb = mk(es_at, "d_colsb", [128, 2], BF16)
        d_rq = mk(es_at, "d_rq", [1, 128], F32)
        d_rqb = mk(es_at, "d_rqb", [1, 128], BF16)
        d_rT = mk(es_at, "d_rT", [1, 256], BF16)
        d_odb = mk(es_at, "d_odb", [128, 256], BF16)
        diagb = mk(es_at, "diagb", [128, 128 * FIRL], BF16)
        diagsb = mk(es_at, "diagsb", [128, 128 * FIRS], BF16)
        vband = mk(es_at, "vband", [128, 63 + L], BF16)
        fir_st = mk(es_at, "fir_st", [128, 2 * T512], BF16)
        od_st = mk(es_at, "od_st", [128, 2 * T512], BF16)
        st_sq = mk(es_at, "st_sq", [128, T512], BF16)
        st_ab = mk(es_at, "st_ab", [128, T512], BF16)
        st_gin = mk(es_at, "st_gin", [128, 256], F32)
        st_acc = mk(es_at, "st_acc", [128, 192], F32)
        st_ginb = mk(es_at, "st_ginb", [128, 256], BF16)
        st_scr = mk(es_at, "st_scr", [128, 64], F32)
        st_all = mk(es_at, "st_all", [16, T512], BF16)
        st_al2 = mk(es_at, "st_al2", [16, T512], BF16)
        h1sh = mk(es_at, "h1sh", [128, CH * T512], BF16)
        h1b = mk(es_at, "h1b", [128, CH * T512], BF16)
        g_sc1 = mk(es_at, "g_sc1", [128, T512], F32)
        g_sc2 = mk(es_at, "g_sc2", [128, T512], F32)
        g_l = mk(es_at, "g_l", [128, 32], F32)
        wgt_b = mk(es_at, "wgt_b", [128, 16], BF16)
        wgtT = mk(es_at, "wgtT", [1, 4 * T512], BF16)
        br_fs = mk(es_at, "br_fs", [128, 2 * T512], BF16)
        br_fl = mk(es_at, "br_fl", [128, 2 * T512], BF16)
        br_od = mk(es_at, "br_od", [128, 2 * T512], BF16)
        br_v = mk(es_at, "br_v", [128, 2 * T512], BF16)
        oc_f = mk(es_at, "oc_f", [128, 2 * T512], F32)
        oc_b = mk(es_at, "oc_b", [128, 2 * T512], BF16)
        on_sb = mk(es_at, "on_sb", [128, CH * T512], BF16)
        rms_c = mk(es_at, "rms_c", [128, 8], F32)
        rms_b = mk(es_at, "rms_b", [128, 8], BF16)
        ao_f = mk(es_at, "ao_f", [128, HS], F32)

        def emit_xnT(tile512):
            for g in range(4):
                t = tile512 * 4 + g
                sp.wait(s_act)
                sp.dma_start(out=xt_f, in_=X[t * 128:(t + 1) * 128, :],
                             inc=[s_ld.inc(16)])
                dve.wait(s_ld)
                dve.tensor_mul(lncols[:, 0:1], M_sb[:, t:t + 1], R_sb[:, t:t + 1])
                dve.drain()
                dve.tensor_scalar(lncols[:, 0:1], lncols[:, 0:1], -1.0, None,
                                  ALU.mult)
                dve.mark(s_dve)
                act.wait(s_dve)
                act.activation(xnb_g, xt_f, AF.Identity, scale=R_sb[:, t:t + 1],
                               bias=lncols[:, 0:1], inc=[s_act.inc()])
                for half in range(2):
                    bank = PB[6 + half]
                    pe.wait(s_act)
                    pe.wait(s_dve)
                    for cc in range(4):
                        c = half * 4 + cc
                        pe.matmul(bank[:, cc * 128:(cc + 1) * 128],
                                  xnb_g[:, c * 128:(c + 1) * 128], kIb,
                                  start=True, stop=True)
                    pe.mark(s_pe)
                    dve.wait(s_pe)
                    for cc in range(4):
                        c = half * 4 + cc
                        dve.tensor_copy(
                            xnT[:, c * T512 + g * 128:c * T512 + (g + 1) * 128],
                            bank[:, cc * 128:(cc + 1) * 128])
                    dve.mark(s_dve)

        for s in range(n_seq):
            dve.wait(s_pe)
            dve.wait(s_act)
            dve.memset(pretail, 0.0)
            dve.memset(d_S, 0.0)
            dve.mark(s_dve)

            for tt in range(n_tiles):
                tile512 = s * NT_SEQ + tt
                tok0 = tile512 * T512
                emit_xnT(tile512)
                n_xnT = s_dve.n

                pe.wait(s_dve, n_xnT)
                for c in range(CH):
                    pe.matmul(PB[5][0:4, 0:T512], WbT[:, c * 4:(c + 1) * 4],
                              xnT[:, c * T512:(c + 1) * T512],
                              start=(c == 0), stop=(c == CH - 1))
                pe.mark(s_pe)
                act.wait(s_pe)
                act.activation(th_f[0:4, 0:T512], PB[5][0:4, 0:T512], AF.Tanh,
                               scale=0.5, inc=[s_act.inc()])
                dve.wait(s_act)
                dve.tensor_scalar(th_f[0:4, 0:T512], th_f[0:4, 0:T512], 0.5, 0.5,
                                  ALU.mult, ALU.add)
                dve.drain()
                dve.tensor_copy(betab, th_f[0:4, 0:T512])
                dve.mark(s_dve)
                pe.wait(s_dve)
                for sub in range(4):
                    pe.matmul(PB[5][:, 128 + sub * 4:128 + (sub + 1) * 4],
                              betab[0:4, sub * 128:(sub + 1) * 128],
                              kIb[0:4, 0:4], start=True, stop=True)
                pe.mark(s_pe)
                act.wait(s_pe)
                act.copy(betaT, PB[5][:, 128:144])
                act.mark(s_act)
                pe.wait(s_act)

                dve.wait(s_gp)
                for xi, outdst in enumerate([qT_t, kT_t, vT_t]):
                    wsrc = [D["wqT_bf"], D["wkT_bf"], D["wvT_bf"]][xi]
                    sp.wait(s_pe)
                    sp.dma_start(out=WstrT[:, 0:CH * HS].rearrange(
                                     "p (c o) -> p c o", o=HS),
                                 in_=wsrc.rearrange("(c p) o -> p c o", p=128),
                                 inc=[s_ld.inc(16)])
                    n_w = s_ld.n
                    dve.wait(s_act)
                    for c in range(CH):
                        dve.tensor_copy(
                            preb[:, c * (3 + T512):c * (3 + T512) + 3],
                            pretail[:, (xi * CH + c) * 3:(xi * CH + c + 1) * 3])
                    dve.mark(s_dve)
                    pe.wait(s_ld, n_w)
                    pe.wait(s_dve)
                    pe.wait(s_act)
                    proj_base = s_act.n
                    for ic in range(CH):
                        bank = PB[ic % 2]
                        pe.wait(s_act, max(proj_base, s_act.n - 1))
                        for kc in range(CH):
                            pe.matmul(bank[:, 0:T512],
                                      WstrT[:, kc * HS + ic * 128:
                                            kc * HS + (ic + 1) * 128],
                                      xnT[:, kc * T512:(kc + 1) * T512],
                                      start=(kc == 0), stop=(kc == CH - 1))
                        pe.mark(s_pe)
                        act.wait(s_pe)
                        act.copy(
                            preb[:, ic * (3 + T512) + 3:(ic + 1) * (3 + T512)],
                            bank[:, 0:T512])
                        act.mark(s_act)
                    n_pre = s_act.n
                    dve.wait(s_act, n_pre)
                    for ic in range(CH):
                        for j in range(CONV):
                            dve.tensor_scalar(
                                cdiag[:, j * 128:(j + 1) * 128], kIb,
                                convw[:, (xi * CH + ic) * CONV + j:
                                      (xi * CH + ic) * CONV + j + 1],
                                None, ALU.mult)
                        dve.tensor_copy(
                            pretail[:, (xi * CH + ic) * 3:(xi * CH + ic + 1) * 3],
                            preb[:, ic * (3 + T512) + T512:
                                 ic * (3 + T512) + T512 + 3])
                        dve.mark(s_dve)
                        pe.wait(s_dve)
                        bank = PB[2 + ic % 2]
                        pe.wait(s_dve)
                        for j in range(CONV):
                            pe.matmul(bank[:, 0:T512],
                                      cdiag[:, j * 128:(j + 1) * 128],
                                      preb[:, ic * (3 + T512) + j:
                                           ic * (3 + T512) + j + T512],
                                      start=(j == 0), stop=(j == CONV - 1))
                        pe.mark(s_pe)
                        act.wait(s_pe)
                        act.activation(th_f, bank[:, 0:T512], AF.Tanh, scale=0.5,
                                       inc=[s_act.inc()])
                        dve.wait(s_act)
                        if xi < 2:
                            dve.scalar_tensor_tensor(
                                outdst[:, ic * T512:(ic + 1) * T512],
                                th_f, 1.0, bank[:, 0:T512], ALU.add, ALU.mult,
                                inc=[s_dve.inc()])
                        else:
                            dve.scalar_tensor_tensor(
                                g_sc1, th_f, 1.0, bank[:, 0:T512],
                                ALU.add, ALU.mult)
                            dve.tensor_scalar(
                                outdst[:, ic * T512:(ic + 1) * T512],
                                g_sc1, 0.5, None, ALU.mult, inc=[s_dve.inc()])
                        pe.wait(s_dve)
                    if xi == 2:
                        gp.wait(s_dve)
                        for c in range(CH):
                            gp.dma_start(
                                out=D["v_d"][c * 128:(c + 1) * 128,
                                             tok0:tok0 + T512],
                                in_=vT_t[:, c * T512:(c + 1) * T512],
                                inc=[s_gp.inc(16)])

                dprev = {"d3": s_dve.n, "a2": s_act.n, "d7": s_dve.n}
                for ci in range(4):
                    for h in range(NH):
                        co = ci * 128
                        qF0 = qT_t[:, (2 * h) * T512 + co:(2 * h) * T512 + co + 128]
                        qF1 = qT_t[:, (2 * h + 1) * T512 + co:
                                   (2 * h + 1) * T512 + co + 128]
                        kF0 = kT_t[:, (2 * h) * T512 + co:(2 * h) * T512 + co + 128]
                        kF1 = kT_t[:, (2 * h + 1) * T512 + co:
                                   (2 * h + 1) * T512 + co + 128]
                        vF0 = vT_t[:, (2 * h) * T512 + co:(2 * h) * T512 + co + 128]
                        vF1 = vT_t[:, (2 * h + 1) * T512 + co:
                                   (2 * h + 1) * T512 + co + 128]
                        bcol = betaT[:, ci * 4 + h:ci * 4 + h + 1]
                        Sreg = d_S[:, h * 512:(h + 1) * 512]

                        pe.wait(s_dve, dprev["d3"])
                        pe.wait(s_act, dprev["a2"])
                        pe.matmul(PB[0][:, 0:128], kF0, kIb, start=True, stop=True)
                        pe.matmul(PB[0][:, 128:256], kF1, kIb, start=True,
                                  stop=True)
                        pe.matmul(PB[0][:, 256:384], vF0, kIb, start=True,
                                  stop=True)
                        pe.matmul(PB[0][:, 384:512], vF1, kIb, start=True,
                                  stop=True)
                        pe.mark(s_pe)
                        act.wait(s_pe)
                        act.wait(s_dve, dprev["d7"])
                        act.activation(d_sq, PB[0][:, 0:256], AF.Square,
                                       accum_out=d_cols[:, 0:1])
                        act.activation(d_cols[:, 1:2], d_cols[:, 0:1], AF.Ln,
                                       bias=eps_c[:, 1:2])
                        act.activation(d_cols[:, 2:3], d_cols[:, 1:2], AF.Exp,
                                       scale=-0.5)
                        act.activation(d_sq[:, 0:128], qF0, AF.Square)
                        act.activation(d_sq[:, 128:256], qF1, AF.Square)
                        dprev["a2"] = act.mark(s_act)
                        dve.wait(s_act)
                        dve.tensor_mul(d_cols[:, 3:4], d_cols[:, 2:3], bcol)
                        dve.drain()
                        dve.tensor_scalar(d_cols[:, 4:5], d_cols[:, 3:4], -1.0,
                                          None, ALU.mult)
                        dve.drain()
                        dve.tensor_copy(d_colsb[:, 0:1], d_cols[:, 2:3])
                        dve.tensor_copy(d_colsb[:, 1:2], d_cols[:, 4:5])
                        dve.tensor_scalar(d_kbT, PB[0][:, 0:256], d_cols[:, 3:4],
                                          None, ALU.mult)
                        dve.tensor_scalar(d_khT, PB[0][:, 0:256], d_cols[:, 2:3],
                                          None, ALU.mult)
                        dve.tensor_scalar(d_vbT, PB[0][:, 256:512], bcol,
                                          None, ALU.mult)
                        dprev["d3"] = dve.mark(s_dve)
                        pe.wait(s_dve)
                        pe.matmul(PB[2][0:1, 256:384], d_colsb[:, 0:1], kIb,
                                  start=True, stop=True)
                        pe.matmul(PB[2][0:1, 384:512], d_colsb[:, 1:2], kIb,
                                  start=True, stop=True)
                        pe.matmul(PB[1][:, 0:128], kF0, kF0, start=True,
                                  stop=False)
                        pe.matmul(PB[1][:, 0:128], kF1, kF1, start=False,
                                  stop=True)
                        pe.matmul(PB[1][:, 128:256], kF0, qF0, start=True,
                                  stop=False)
                        pe.matmul(PB[1][:, 128:256], kF1, qF1, start=False,
                                  stop=True)
                        pe.matmul(PB[1][0:1, 384:512], kOnesC, d_sq[:, 0:128],
                                  start=True, stop=False)
                        pe.matmul(PB[1][0:1, 384:512], kOnesC, d_sq[:, 128:256],
                                  start=False, stop=True)
                        pe.mark(s_pe)
                        act.wait(s_pe)
                        act.activation(d_rq, PB[1][0:1, 384:512], AF.Ln,
                                       bias=eps_c[0:1, 1:2])
                        act.activation(d_rqb, d_rq, AF.Exp, scale=-0.5)
                        act.copy(d_rT[0:1, 0:128], PB[2][0:1, 256:384])
                        act.copy(d_rT[0:1, 128:256], PB[2][0:1, 384:512])
                        act.mark(s_act)
                        pe.wait(s_act)
                        pe.matmul(PB[2][:, 0:128], kOnesR, d_rT[0:1, 128:256],
                                  start=True, stop=True)
                        pe.matmul(PB[2][:, 128:256], kOnesR, d_rT[0:1, 0:128],
                                  start=True, stop=True)
                        pe.matmul(PB[2][:, 256:384], kOnesR, d_rqb,
                                  start=True, stop=True)
                        pe.mark(s_pe)
                        dve.wait(s_pe)
                        dve.tensor_tensor(d_Bup, PB[2][:, 0:128], kMups, ALU.mult)
                        dve.tensor_tensor(d_Blo, PB[2][:, 128:256], kMlos,
                                          ALU.mult)
                        dve.tensor_tensor(d_qh[:, 0:128], qF0, PB[2][:, 256:384],
                                          ALU.mult)
                        dve.tensor_tensor(d_qh[:, 128:256], qF1, PB[2][:, 256:384],
                                          ALU.mult)
                        dve.scalar_tensor_tensor(d_U, PB[1][:, 0:128],
                                                 d_cols[:, 2:3], d_Bup,
                                                 ALU.mult, ALU.mult)
                        dve.scalar_tensor_tensor(d_L, PB[1][:, 0:128],
                                                 d_cols[:, 4:5], d_Blo,
                                                 ALU.mult, ALU.mult)
                        dve.tensor_tensor(d_mrq, PB[2][:, 256:384], kMupi,
                                          ALU.mult)
                        dve.drain()
                        dve.scalar_tensor_tensor(d_att, PB[1][:, 128:256],
                                                 d_cols[:, 2:3], d_mrq,
                                                 ALU.mult, ALU.mult)
                        dve.drain()
                        dve.tensor_tensor(d_P, d_U, kIb, ALU.add)
                        dprev["d7"] = dve.mark(s_dve)
                        for kk in range(1, 7):
                            pe.wait(s_dve)
                            pe.wait(s_act)
                            pe.matmul(PB[3][:, 0:128], d_L, d_U, start=True,
                                      stop=True)
                            pe.matmul(PB[2][:, 0:128], d_U, d_L, start=True,
                                      stop=True)
                            pe.matmul(PB[4][:, 0:128], kIb, d_P, start=True,
                                      stop=False)
                            pe.mark(s_pe)
                            act.wait(s_pe)
                            act.copy(d_U, PB[3][:, 0:128])
                            act.mark(s_act)
                            dve.wait(s_pe)
                            dve.tensor_copy(d_L, PB[2][:, 0:128])
                            dve.mark(s_dve)
                            pe.wait(s_act)
                            pe.wait(s_dve)
                            pe.matmul(PB[4][:, 0:128], d_L, d_P, start=False,
                                      stop=True)
                            pe.mark(s_pe)
                            dve.wait(s_pe)
                            dve.tensor_copy(d_P, PB[4][:, 0:128])
                            dve.mark(s_dve)
                        pe.wait(s_dve)
                        pe.matmul(PB[4][:, 128:256], d_kbT[:, 0:128], d_P,
                                  start=True, stop=True)
                        pe.matmul(PB[4][:, 256:384], d_kbT[:, 128:256], d_P,
                                  start=True, stop=True)
                        pe.mark(s_pe)
                        act.wait(s_pe)
                        act.activation(d_wTn[:, 0:128], PB[4][:, 128:256],
                                       AF.Copy, scale=-1.0)
                        act.activation(d_wTn[:, 128:256], PB[4][:, 256:384],
                                       AF.Copy, scale=-1.0)
                        act.mark(s_act)
                        dve.tensor_copy(d_Sb, Sreg)
                        dve.mark(s_dve)
                        pe.wait(s_act)
                        pe.wait(s_dve)
                        pe.matmul(PB[5][:, 0:256], d_P, d_vbT, start=True,
                                  stop=False)
                        pe.matmul(PB[5][:, 0:256], d_wTn[:, 0:128],
                                  d_Sb[:, 0:256], start=False, stop=False)
                        pe.matmul(PB[5][:, 0:256], d_wTn[:, 128:256],
                                  d_Sb[:, 256:512], start=False, stop=True)
                        pe.mark(s_pe)
                        dve.wait(s_pe)
                        dve.tensor_copy(d_u, PB[5][:, 0:256])
                        dve.mark(s_dve)
                        pe.wait(s_dve)
                        pe.matmul(PB[6][:, 0:128], d_Sb[:, 0:128],
                                  d_qh[:, 0:128], start=True, stop=False)
                        pe.matmul(PB[6][:, 0:128], d_Sb[:, 256:384],
                                  d_qh[:, 128:256], start=False, stop=False)
                        pe.matmul(PB[6][:, 0:128], d_u[:, 0:128], d_att,
                                  start=False, stop=True)
                        pe.matmul(PB[6][:, 128:256], d_Sb[:, 128:256],
                                  d_qh[:, 0:128], start=True, stop=False)
                        pe.matmul(PB[6][:, 128:256], d_Sb[:, 384:512],
                                  d_qh[:, 128:256], start=False, stop=False)
                        pe.matmul(PB[6][:, 128:256], d_u[:, 128:256], d_att,
                                  start=False, stop=True)
                        pe.mark(s_pe)
                        act.wait(s_pe)
                        act.wait(s_gp)
                        act.copy(d_odb, PB[6][:, 0:256])
                        act.drain()
                        act.mark(s_act)
                        gp.wait(s_act)
                        gp.dma_start(out=D["odT_d"][2 * h * 128:(2 * h + 1) * 128,
                                                    tok0 + co:tok0 + co + 128],
                                     in_=d_odb[:, 0:128], inc=[s_gp.inc(16)])
                        gp.dma_start(
                            out=D["odT_d"][(2 * h + 1) * 128:(2 * h + 2) * 128,
                                           tok0 + co:tok0 + co + 128],
                            in_=d_odb[:, 128:256], inc=[s_gp.inc(16)])
                        pe.matmul(PB[7][:, 0:256], d_khT[:, 0:128], d_u,
                                  start=True, stop=True)
                        pe.matmul(PB[7][:, 256:512], d_khT[:, 128:256], d_u,
                                  start=True, stop=True)
                        pe.mark(s_pe)
                        dve.wait(s_pe)
                        dve.tensor_tensor(Sreg[:, 0:256], Sreg[:, 0:256],
                                          PB[7][:, 0:256], ALU.add)
                        dve.tensor_tensor(Sreg[:, 256:512], Sreg[:, 256:512],
                                          PB[7][:, 256:512], ALU.add)
                        dve.mark(s_dve)
                        pe.wait(s_dve)
                        if dbg_delta and s == 0 and tt == 0 and ci == 0 and h == 0:
                            gp.wait(s_dve)
                            for nm_, ap_, off_ in [
                                ("U", d_U, 0), ("L", d_L, 128), ("P", d_P, 256),
                                ("att", d_att, 384), ("kbT", d_kbT, 512),
                                ("vbT", d_vbT, 768), ("u", d_u, 1024),
                                ("qh", d_qh, 1280), ("khT", d_khT, 1536),
                                ("wTn", d_wTn, 1792), ("Sb", d_Sb, 2048),
                                ("odb", d_odb, 2560), ("rqb", d_rqb, 2816),
                            ]:
                                w_ = ap_.shape[-1] if hasattr(ap_, "shape") else 128
                                gp.dma_start(out=DBG["tile2"][0:ap_.shape[0],
                                                              off_:off_ + ap_.shape[1]],
                                    in_=ap_, inc=[s_gp.inc(16)])
                            gp.dma_start(out=DBG["tile3"][:, 0:16],
                                         in_=d_cols.bitcast(BF16), inc=[s_gp.inc(16)])
                            gp.dma_start(out=DBG["tile3"][:, 16:144], in_=d_Bup,
                                         inc=[s_gp.inc(16)])
                            gp.dma_start(out=DBG["tile3"][:, 144:272], in_=d_Blo,
                                         inc=[s_gp.inc(16)])
                            gp.wait(s_gp)
                            barrier()

            if lim >= 3:
                sp.wait(s_gp)
                vb_pe = {0: s_pe.n, 1: s_pe.n}
                vb_act = {0: s_act.n, 1: s_act.n}
                od_pe = {0: s_pe.n, 1: s_pe.n}
                od_act = {0: s_act.n, 1: s_act.n}
                ch_i = 0
                sli = 0
                for h in range(NH):
                    stat_base_done = False
                    for c2loc in range(2):
                        c2 = 2 * h + c2loc
                        dve.wait(s_pe)
                        for j in range(FIRL):
                            dve.tensor_scalar(diagb[:, j * 128:(j + 1) * 128], kIb,
                                              firw[:, CH * FIRS + c2 * FIRL + j:
                                                   CH * FIRS + c2 * FIRL + j + 1],
                                              None, ALU.mult)
                        for j in range(FIRS):
                            dve.tensor_scalar(diagsb[:, j * 128:(j + 1) * 128], kIb,
                                              firw[:, c2 * FIRS + j:
                                                   c2 * FIRS + j + 1],
                                              None, ALU.mult)
                        p = 0
                        vb = vband[:, p * (63 + L):(p + 1) * (63 + L)]
                        dve.memset(vb[:, 0:63], 0.0)
                        dve.mark(s_dve)
                        sp.wait(s_pe, vb_pe[p])
                        sp.wait(s_act, vb_act[p])
                        sp.dma_start(out=vb[:, 63:63 + L],
                                     in_=D["v_d"][c2 * 128:(c2 + 1) * 128,
                                                  s * L:(s + 1) * L],
                                     inc=[s_ld.inc(16)])
                        n_vb = s_ld.n
                        for sl in range(NT_SEQ):
                            t0 = sl * T512
                            tok0 = s * L + t0
                            pe.wait(s_ld, n_vb)
                            pe.wait(s_dve)
                            pe.wait(s_act)
                            for j in range(FIRS):
                                pe.matmul(PB[0][:, 0:T512],
                                          diagsb[:, j * 128:(j + 1) * 128],
                                          vb[:, t0 + 59 + j:t0 + 59 + j + T512],
                                          start=(j == 0), stop=(j == FIRS - 1))
                            for j in range(FIRL):
                                pe.matmul(PB[1][:, 0:T512],
                                          diagb[:, j * 128:(j + 1) * 128],
                                          vb[:, t0 + j:t0 + j + T512],
                                          start=(j == 0), stop=(j == FIRL - 1))
                            pe.mark(s_pe)
                            act.wait(s_pe)
                            act.wait(s_gp)
                            act.copy(fir_st[:, 0:T512], PB[0][:, 0:T512])
                            act.copy(fir_st[:, T512:2 * T512],
                                            PB[1][:, 0:T512])
                            act.drain()
                            act.mark(s_act)
                            gp.wait(s_act)
                            gp.dma_start(out=D["fs_d"][c2 * 128:(c2 + 1) * 128,
                                                       tok0:tok0 + T512],
                                         in_=fir_st[:, 0:T512], inc=[s_gp.inc(16)])
                            gp.dma_start(out=D["fl_d"][c2 * 128:(c2 + 1) * 128,
                                                       tok0:tok0 + T512],
                                         in_=fir_st[:, T512:2 * T512],
                                         inc=[s_gp.inc(16)])
                            q = sli % 2
                            odv = od_st[:, q * T512:(q + 1) * T512]
                            sp.wait(s_pe, od_pe[q])
                            sp.wait(s_act, od_act[q])
                            sp.dma_start(out=odv,
                                         in_=D["odT_d"][c2 * 128:(c2 + 1) * 128,
                                                        tok0:tok0 + T512],
                                         inc=[s_ld.inc(16)])
                            srcs = [fir_st[:, 0:T512], fir_st[:, T512:2 * T512],
                                    odv, vb[:, 63 + t0:63 + t0 + T512]]
                            dve.wait(s_ld)
                            dve.wait(s_act)
                            for ti, srcap in enumerate(srcs):
                                act.activation(st_sq, srcap, AF.Square)
                                act.activation(st_ab, srcap, AF.Abs)
                                act.mark(s_act)
                                dve.wait(s_act)
                                dve.mark(s_dve)
                                pe.wait(s_dve)
                                for sub in range(4):
                                    base_col = (sl * 4 + sub) * 12 + ti * 3
                                    pe.matmul(PB[4][:, base_col:base_col + 1],
                                              srcap[:, sub * 128:(sub + 1) * 128],
                                              kOnesC, start=True, stop=True)
                                    pe.matmul(PB[4][:, base_col + 1:base_col + 2],
                                              st_sq[:, sub * 128:(sub + 1) * 128],
                                              kOnesC, start=True, stop=True)
                                    pe.matmul(PB[4][:, base_col + 2:base_col + 3],
                                              st_ab[:, sub * 128:(sub + 1) * 128],
                                              kOnesC, start=True, stop=True)
                                pe.mark(s_pe)
                                act.wait(s_pe)
                                dve.wait(s_pe)
                            od_pe[q] = s_pe.n
                            od_act[q] = s_act.n
                            sli += 1
                        vb_pe[p] = s_pe.n
                        vb_act[p] = s_act.n
                        ch_i += 1
                        dve.wait(s_pe)
                        if c2loc == 0:
                            dve.tensor_copy(st_acc, PB[4][:, 0:192])
                        else:
                            dve.tensor_tensor(st_acc, st_acc, PB[4][:, 0:192],
                                              ALU.add)
                        dve.drain()
                        dve.mark(s_dve)
                        pe.wait(s_dve)
                    n_stats = s_pe.n
                    inv_dv = 1.0 / DV
                    IN3 = st_acc.rearrange("p (n k) -> p n k", k=3)
                    OUT4 = st_gin.rearrange("p (n s) -> p n s", s=4)
                    scr_a = g_sc1[:, 0:64].rearrange("p (n s) -> p n s", s=1)
                    scr_b = g_sc2[:, 0:64].rearrange("p (n s) -> p n s", s=1)
                    dve.wait(s_pe, n_stats)
                    dve.tensor_scalar(OUT4[:, :, 0:1], IN3[:, :, 0:1], inv_dv, None,
                                      ALU.mult)
                    dve.tensor_scalar(scr_a, IN3[:, :, 1:2], inv_dv, None,
                                      ALU.mult)
                    dve.tensor_scalar(OUT4[:, :, 2:3], IN3[:, :, 2:3], inv_dv, None,
                                      ALU.mult)
                    dve.tensor_scalar(scr_b, OUT4[:, :, 0:1], -1.0, None,
                                      ALU.mult)
                    dve.drain()
                    dve.tensor_tensor(scr_b, scr_b, OUT4[:, :, 0:1], ALU.mult)
                    dve.drain()
                    dve.tensor_tensor(OUT4[:, :, 1:2], scr_b, scr_a, ALU.add)
                    dve.mark(s_dve)
                    act.wait(s_dve)
                    act.activation(scr_a, IN3[:, :, 1:2], AF.Ln, bias=eps_c[:, 2:3])
                    act.activation(OUT4[:, :, 3:4], scr_a, AF.Exp, scale=0.5)
                    act.mark(s_act)
                    dve.wait(s_act)
                    dve.drain()
                    dve.tensor_copy(st_ginb, st_gin)
                    dve.mark(s_dve)
                    for tg in range(16):
                        pe.wait(s_dve)
                        bank = PB[5]
                        pe.matmul(bank[0:16, (tg % 4) * 128:(tg % 4 + 1) * 128],
                                  st_ginb[:, tg * 16:(tg + 1) * 16], kIb,
                                  start=True, stop=True)
                        if tg % 4 == 3:
                            pe.mark(s_pe)
                            act.wait(s_pe)
                            act.wait(s_gp)
                            act.copy(st_all[0:16, 0:512], bank[0:16, 0:512])
                            act.drain()
                            act.mark(s_act)
                            gp.wait(s_act)
                            gp.dma_start(
                                out=D["stall_d"][0:16, h * L + (tg - 3) * 128:
                                                 h * L + (tg + 1) * 128],
                                in_=st_all[0:16, 0:512], inc=[s_gp.inc(16)])
                            pe.wait(s_act)

            if lim >= 4:
                for sl in range(NT_SEQ):
                    tile512 = s * NT_SEQ + sl
                    tok0 = tile512 * T512
                    emit_xnT(tile512)
                    n_xnT = s_dve.n
                    pe.wait(s_dve, n_xnT)
                    for oc in range(CH):
                        bank = PB[oc % 2]
                        pe.wait(s_act)
                        for kc in range(CH):
                            pe.matmul(bank[:, 0:T512],
                                      W1aT[:, kc * HS + oc * 128:
                                           kc * HS + (oc + 1) * 128],
                                      xnT[:, kc * T512:(kc + 1) * T512],
                                      start=(kc == 0), stop=(kc == CH - 1))
                        pe.mark(s_pe)
                        act.wait(s_pe)
                        act.activation(h1sh[:, oc * T512:(oc + 1) * T512],
                                       bank[:, 0:T512], AF.Identity,
                                       bias=b1c[:, oc:oc + 1])
                        act.mark(s_act)
                    n_h1sh = s_act.n

                    for h in range(NH):
                        sp.wait(s_pe)
                        sp.dma_start(out=st_al2[0:16, 0:T512],
                                     in_=D["stall_d"][0:16, h * L + sl * T512:
                                                      h * L + (sl + 1) * T512],
                                     inc=[s_ld.inc(16)])
                        n_st = s_ld.n
                        dve.wait(s_act, n_h1sh)
                        pe.wait(s_ld, n_st)
                        for oc in range(CH):
                            bank = PB[2 + oc % 2]
                            pe.wait(s_dve)
                            pe.matmul(bank[:, 0:T512],
                                      W1bT[:, oc * 128:(oc + 1) * 128],
                                      st_al2[0:16, 0:T512],
                                      start=True, stop=True)
                            pe.mark(s_pe)
                            dve.wait(s_pe)
                            dve.scalar_tensor_tensor(
                                g_sc1, h1sh[:, oc * T512:(oc + 1) * T512], 1.0,
                                bank[:, 0:T512], ALU.mult, ALU.add)
                            dve.mark(s_dve)
                            act.wait(s_dve)
                            act.activation(g_sc2, g_sc1, AF.Square)
                            act.mark(s_act)
                            dve.wait(s_act)
                            dve.tensor_scalar(g_sc2, g_sc2, GC * GA, GC,
                                              ALU.mult, ALU.add)
                            dve.drain()
                            dve.tensor_mul(g_sc2, g_sc1, g_sc2)
                            dve.mark(s_dve)
                            act.wait(s_dve)
                            act.activation(g_sc2, g_sc2, AF.Tanh)
                            act.mark(s_act)
                            dve.wait(s_act)
                            dve.scalar_tensor_tensor(
                                h1b[:, oc * T512:(oc + 1) * T512], g_sc2, 1.0,
                                g_sc1, ALU.add, ALU.mult, inc=[s_dve.inc()])
                        pe.wait(s_dve)
                        for sub in range(4):
                            for oc in range(CH):
                                pe.matmul(PB[4][:, sub * 4:(sub + 1) * 4],
                                          h1b[:, oc * T512 + sub * 128:
                                              oc * T512 + (sub + 1) * 128],
                                          W2T[:, oc * 4:(oc + 1) * 4],
                                          start=(oc == 0), stop=(oc == CH - 1))
                        pe.mark(s_pe)
                        dve.wait(s_pe)
                        dve.tensor_tensor(
                            g_l[:, 0:16].rearrange("p (s g) -> p s g", g=4),
                            PB[4][:, 0:16].rearrange("p (s g) -> p s g", g=4),
                            biasbc[:, 4 * h:4 * h + 4].rearrange(
                                "p (o g) -> p o g", o=1).broadcast_to([128, 4, 4]), ALU.add)
                        dve.drain()
                        dve.tensor_scalar(g_l[:, 0:16], g_l[:, 0:16],
                                          rtb[:, h:h + 1], None, ALU.mult)
                        dve.drain()
                        dve.reduce_max(g_l[:, 16:20],
                                       g_l[:, 0:16].rearrange("p (s g) -> p s g",
                                                              g=4), axis=AX.X)
                        dve.tensor_tensor(
                            g_l[:, 0:16].rearrange("p (s g) -> p s g", g=4),
                            g_l[:, 0:16].rearrange("p (s g) -> p s g", g=4),
                            g_l[:, 16:20].rearrange("p (s o) -> p s o", o=1).broadcast_to(
                                [128, 4, 4]), ALU.subtract)
                        dve.mark(s_dve)
                        act.wait(s_dve)
                        act.activation(g_l[:, 0:16], g_l[:, 0:16], AF.Exp)
                        act.mark(s_act)
                        dve.wait(s_act)
                        dve.reduce_sum(g_l[:, 20:24],
                                       g_l[:, 0:16].rearrange("p (s g) -> p s g",
                                                              g=4), axis=AX.X)
                        dve.drain()
                        dve.reciprocal(g_l[:, 24:28], g_l[:, 20:24])
                        dve.drain()
                        dve.tensor_tensor(
                            g_l[:, 0:16].rearrange("p (s g) -> p s g", g=4),
                            g_l[:, 0:16].rearrange("p (s g) -> p s g", g=4),
                            g_l[:, 24:28].rearrange("p (s o) -> p s o", o=1).broadcast_to(
                                [128, 4, 4]), ALU.mult)
                        dve.drain()
                        dve.tensor_scalar(wgt_b, g_l[:, 0:16],
                                          1.0 - 4.0 * EPS_FLOOR, EPS_FLOOR,
                                          ALU.mult, ALU.add)
                        dve.mark(s_dve)
                        pe.wait(s_dve)
                        pe.wait(s_act)
                        for g in range(4):
                            for sub in range(4):
                                pe.matmul(PB[5][0:1, sub * 128:(sub + 1) * 128],
                                          wgt_b[:, sub * 4 + g:sub * 4 + g + 1], kIb,
                                          start=True, stop=True)
                            pe.mark(s_pe)
                            act.wait(s_pe)
                            act.copy(wgtT[0:1, g * T512:(g + 1) * T512],
                                            PB[5][0:1, 0:T512])
                            act.mark(s_act)
                            pe.wait(s_act)
                        sp.wait(s_dve)
                        for c2loc in range(2):
                            c2 = 2 * h + c2loc
                            for dst, dsrc in [(br_fs, "fs_d"), (br_fl, "fl_d"),
                                              (br_od, "odT_d"), (br_v, "v_d")]:
                                sp.dma_start(
                                    out=dst[:, c2loc * T512:(c2loc + 1) * T512],
                                    in_=D[dsrc][c2 * 128:(c2 + 1) * 128,
                                                tok0:tok0 + T512],
                                    inc=[s_ld.inc(16)])
                        n_br = s_ld.n
                        pe.wait(s_act)
                        for g, br in enumerate([br_fs, br_fl, br_od, br_v]):
                            pe.wait(s_dve)
                            pe.matmul(PB[6][:, 0:T512], kOnesR,
                                      wgtT[0:1, g * T512:(g + 1) * T512],
                                      start=True, stop=True)
                            pe.mark(s_pe)
                            dve.wait(s_pe)
                            dve.wait(s_ld, n_br)
                            for c2loc in range(2):
                                seg = slice(c2loc * T512, (c2loc + 1) * T512)
                                if g == 0:
                                    dve.tensor_tensor(oc_f[:, seg], br[:, seg],
                                                      PB[6][:, 0:T512], ALU.mult)
                                else:
                                    dve.tensor_tensor(g_sc1, br[:, seg],
                                                      PB[6][:, 0:T512], ALU.mult)
                                    dve.drain()
                                    dve.tensor_tensor(oc_f[:, seg], oc_f[:, seg],
                                                      g_sc1, ALU.add)
                            dve.mark(s_dve)
                        act.wait(s_dve)
                        act.activation(oc_b[:, 0:T512], oc_f[:, 0:T512], AF.Square)
                        act.activation(oc_b[:, T512:2 * T512], oc_f[:, T512:2 * T512],
                                       AF.Square)
                        act.mark(s_act)
                        pe.wait(s_act)
                        for sub in range(4):
                            pe.matmul(PB[7][:, sub:sub + 1],
                                      oc_b[:, sub * 128:(sub + 1) * 128], kOnesC,
                                      start=True, stop=False)
                            pe.matmul(PB[7][:, sub:sub + 1],
                                      oc_b[:, T512 + sub * 128:
                                           T512 + (sub + 1) * 128], kOnesC,
                                      start=False, stop=True)
                        pe.mark(s_pe)
                        act.wait(s_pe)
                        act.activation(rms_c[:, 0:4], PB[7][:, 0:4], AF.Ln,
                                       scale=1.0 / DV, bias=eps_c[:, 0:1])
                        act.activation(rms_c[:, 0:4], rms_c[:, 0:4], AF.Exp,
                                       scale=-0.5)
                        act.mark(s_act)
                        dve.wait(s_act)
                        dve.tensor_copy(rms_b[:, 0:4], rms_c[:, 0:4])
                        dve.mark(s_dve)
                        pe.wait(s_dve)
                        for sub in range(4):
                            pe.matmul(PB[5][0:1, sub * 128:(sub + 1) * 128],
                                      rms_b[:, sub:sub + 1], kIb,
                                      start=True, stop=True)
                        pe.mark(s_pe)
                        act.wait(s_pe)
                        act.copy(wgtT[0:1, 0:T512], PB[5][0:1, 0:T512])
                        act.mark(s_act)
                        pe.wait(s_act)
                        pe.matmul(PB[6][:, 0:T512], kOnesR, wgtT[0:1, 0:T512],
                                  start=True, stop=True)
                        pe.mark(s_pe)
                        dve.wait(s_pe)
                        for c2loc in range(2):
                            dve.scalar_tensor_tensor(
                                on_sb[:, (2 * h + c2loc) * T512:
                                      (2 * h + c2loc + 1) * T512],
                                oc_f[:, c2loc * T512:(c2loc + 1) * T512],
                                onwc[:, c2loc:c2loc + 1], PB[6][:, 0:T512],
                                ALU.mult, ALU.mult)
                        dve.mark(s_dve)
                        pe.wait(s_dve)

                    for sub in range(4):
                        bank = PB[sub % 2]
                        act.wait(s_ld)
                        for nh2 in range(2):
                            pe.wait(s_act)
                            for c in range(CH):
                                pe.matmul(bank[:, 0:T512],
                                          on_sb[:, c * T512 + sub * 128:
                                                c * T512 + (sub + 1) * 128],
                                          WoT[:, c * HS + nh2 * T512:
                                              c * HS + (nh2 + 1) * T512],
                                          start=(c == 0), stop=(c == CH - 1))
                            pe.mark(s_pe)
                            act.wait(s_pe)
                            act.copy(ao_f[:, nh2 * T512:(nh2 + 1) * T512],
                                            bank[:, 0:T512])
                            act.drain()
                            act.mark(s_act)
                        sp.wait(s_act)
                        sp.dma_start(out=D["attn_d"][tok0 + sub * 128:
                                                     tok0 + (sub + 1) * 128, :],
                                     in_=ao_f, inc=[s_ld.inc(16)])

        if dbg and not dbg_delta:
            barrier()
            sp.dma_start(out=DBG["tile"], in_=qT_t, inc=[s_ld.inc(16)])
            sp.dma_start(out=DBG["tile2"], in_=kT_t, inc=[s_ld.inc(16)])
            sp.dma_start(out=DBG["tile3"], in_=vT_t, inc=[s_ld.inc(16)])
            sp.wait(s_ld, s_ld.n)
        barrier()
        es_at.close()
        es_at_res.close()

    if lim >= 5:
        es_f = ExitStack()
        kOnesRf = mk(es_f, "kOnesRf", [1, 128], F32)
        f_xt = mk(es_f, "f_xt", [128, HS], F32)
        f_i32 = mk(es_f, "f_i32", [128, HS], I32)
        f_xq = mk(es_f, "f_xq", [128, HS], BF16)
        f_xqT = mk(es_f, "f_xqT", [128, CH * T512], BF16)
        f_wgT = mk(es_f, "f_wgT", [128, CH * II], BF16)
        f_wuT = mk(es_f, "f_wuT", [128, CH * II], BF16)
        f_th = mk(es_f, "f_th", [128, 2 * T512], F32)
        f_t1 = mk(es_f, "f_t1", [128, 2 * T512], F32)
        f_up = mk(es_f, "f_up", [128, 2 * T512], F32)
        f_hp = mk(es_f, "f_hp", [128, 2 * T512], F32)
        f_cols = mk(es_f, "f_cols", [128, 4], F32)

        dve.memset(kOnesRf, 1.0)
        dve.mark(s_dve)

        gp.wait(s_cc, cc1_done)
        gp.dma_start(out=misc3[0:1, 0:1], in_=D["cc1_out"], inc=[s_gp.inc(16)])
        act.wait(s_gp)
        act.activation(misc3[0:1, 1:2], smallrows[0:1, 12:13], AF.Tanh, scale=0.5)
        act.mark(s_act)
        dve.wait(s_act)
        dve.tensor_scalar(misc3[0:1, 1:2], misc3[0:1, 1:2], 0.5, 0.5,
                          ALU.mult, ALU.add)
        dve.drain()
        dve.tensor_mul(misc3[0:1, 2:3], misc3[0:1, 1:2], misc3[0:1, 0:1])
        dve.drain()
        dve.tensor_scalar(misc3[0:1, 2:3], misc3[0:1, 2:3], 1.0 / QMAX, None,
                          ALU.mult)
        dve.drain()
        dve.reciprocal(misc3[0:1, 3:4], misc3[0:1, 2:3])
        dve.tensor_mul(misc3[0:1, 4:5], misc3[0:1, 2:3], misc3[0:1, 2:3])
        dve.drain()
        dve.tensor_scalar(misc3[0:1, 4:5], misc3[0:1, 4:5], 0.5, None,
                          ALU.mult)
        dve.drain()
        dve.tensor_scalar(misc3[0:1, 5:6], misc3[0:1, 2:3], 0.5, None,
                          ALU.mult)
        dve.mark(s_dve)
        gp.wait(s_dve)
        gp.dma_start(out=D["sc_d"][0:1, 0:3], in_=misc3[0:1, 3:6],
                     inc=[s_gp.inc(16)])
        sp.wait(s_gp)
        for i in range(3):
            sp.dma_start(out=sx_c[:, i:i + 1],
                         in_=D["sc_d"][0:1, i:i + 1].broadcast_to([128, 1]),
                         inc=[s_ld.inc(16)])
        act.wait(s_ld)
        act.mark(s_act)

        for kc in range(CH):
            sp.dma_start_transpose(f_wgT[:, kc * II:(kc + 1) * II],
                                   D["wg_bf"][:, kc * 128:(kc + 1) * 128],
                                   inc=[s_ld.inc(16)])
            sp.dma_start_transpose(f_wuT[:, kc * II:(kc + 1) * II],
                                   D["wu_bf"][:, kc * 128:(kc + 1) * 128],
                                   inc=[s_ld.inc(16)])
        n_wgu = s_ld.n

        hp_guard = [0, 0]
        for tile512 in range(TOK // T512):
            tok0 = tile512 * T512
            for g in range(4):
                t = tile512 * 4 + g
                sp.wait(s_dve)
                sp.dma_start(out=f_xt, in_=X[t * 128:(t + 1) * 128, :],
                             inc=[s_ld.inc(16)])
                dve.wait(s_ld)
                dve.wait(s_act)
                dve.tensor_tensor(f_cols[:, 0:1], R_sb[:, t:t + 1], sx_c[:, 0:1],
                                  ALU.mult)
                dve.drain()
                dve.tensor_mul(f_cols[:, 1:2], M_sb[:, t:t + 1], f_cols[:, 0:1])
                dve.drain()
                dve.tensor_scalar(f_cols[:, 1:2], f_cols[:, 1:2], -1.0, None,
                                  ALU.mult)
                dve.mark(s_dve)
                act.wait(s_dve)
                act.wait(s_pe)
                act.activation(f_i32, f_xt, AF.Identity, scale=f_cols[:, 0:1],
                               bias=f_cols[:, 1:2])
                act.drain()
                act.copy(f_xq, f_i32)
                act.mark(s_act)
                dve.wait(s_act)
                dve.tensor_scalar(f_xq, f_xq, QMAX, -QMAX - 1.0, ALU.min,
                                  ALU.max)
                dve.mark(s_dve)
                for half in range(2):
                    bank = PB[6 + half]
                    pe.wait(s_dve)
                    for cc in range(4):
                        c = half * 4 + cc
                        pe.matmul(bank[:, cc * 128:(cc + 1) * 128],
                                  f_xq[:, c * 128:(c + 1) * 128], kIb,
                                  start=True, stop=True)
                    pe.mark(s_pe)
                    dve.wait(s_pe)
                    for cc in range(4):
                        c = half * 4 + cc
                        dve.tensor_copy(
                            f_xqT[:, c * T512 + g * 128:c * T512 + (g + 1) * 128],
                            bank[:, cc * 128:(cc + 1) * 128])
                    dve.mark(s_dve)
            n_xqT = s_dve.n
            pe.wait(s_dve, n_xqT)
            pe.wait(s_ld, n_wgu)
            ffa_dve = s_dve.n
            ffa_act = s_act.n
            pe.wait(s_dve)
            pe.wait(s_act)
            for ic in range(CHI):
                gbank = PB[ic % 2]
                ubank = PB[2 + ic % 2]
                pe.wait(s_dve, max(ffa_dve, s_dve.n - 1))
                pe.wait(s_act, max(ffa_act, s_act.n - 1))
                for kc in range(CH):
                    pe.matmul(gbank[:, 0:T512],
                              f_wgT[:, kc * II + ic * 128:kc * II + (ic + 1) * 128],
                              f_xqT[:, kc * T512:(kc + 1) * T512],
                              start=(kc == 0), stop=(kc == CH - 1))
                pe.mark(s_pe)
                n_g = s_pe.n
                for kc in range(CH):
                    pe.matmul(ubank[:, 0:T512],
                              f_wuT[:, kc * II + ic * 128:kc * II + (ic + 1) * 128],
                              f_xqT[:, kc * T512:(kc + 1) * T512],
                              start=(kc == 0), stop=(kc == CH - 1))
                pe.mark(s_pe)
                seg = slice((ic % 2) * T512, (ic % 2 + 1) * T512)
                act.wait(s_pe, n_g)
                act.activation(f_th[:, seg], gbank[:, 0:T512], AF.Tanh,
                               scale=sx_c[:, 2:3])
                act.wait(s_pe)
                act.activation(f_up[:, seg], ubank[:, 0:T512], AF.Identity,
                               scale=sx_c[:, 1:2])
                act.mark(s_act)
                dve.wait(s_act)
                hp_half = f_hp[:, (ic % 2) * T512:(ic % 2 + 1) * T512]
                dve.wait(s_gp, hp_guard[ic % 2])
                dve.scalar_tensor_tensor(f_t1[:, seg], f_th[:, seg], 1.0,
                                         gbank[:, 0:T512],
                                         ALU.add, ALU.mult)
                dve.drain()
                dve.tensor_mul(hp_half, f_t1[:, seg], f_up[:, seg])
                dve.drain()
                dve.reduce_max(f_cols[:, 2:3], hp_half, axis=AX.X,
                               apply_absolute_value=True)
                dve.drain()
                dve.tensor_tensor(hmax_sb, hmax_sb, f_cols[:, 2:3], ALU.max)
                dve.mark(s_dve)
                gp.wait(s_dve)
                gp.dma_start(out=D["hpre_d"][ic * 128:(ic + 1) * 128,
                                             tok0:tok0 + T512],
                             in_=hp_half, inc=[s_gp.inc(16)])
                hp_guard[ic % 2] = s_gp.n
        n_ffna = s_gp.n

        gp.wait(s_dve)
        gp.dma_start(out=D["bounce1"], in_=hmax_sb, inc=[s_gp.inc(16)])
        gp.wait(s_gp)
        gp.dma_start(out=row128,
                     in_=D["bounce1"].rearrange("(a k) o -> a (k o)", a=1),
                     inc=[s_gp.inc(16)])
        dve.wait(s_gp)
        dve.reduce_max(misc3[0:1, 8:9], row128, axis=AX.X,
                       inc=[s_dve.inc()])
        gp.wait(s_dve)
        gp.dma_start(out=D["cc2_in"], in_=misc3[0:1, 8:9], inc=[s_gp.inc(16)])
        gp.wait(s_gp)
        gp.collective_compute("AllReduce", ALU.max,
                              replica_groups=[list(range(8))],
                              ins=[D["cc2_in"].opt()], outs=[D["cc2_out"].opt()],
                              inc=[s_cc.inc()])
        gp.wait(s_cc)
        gp.dma_start(out=misc3[0:1, 9:10], in_=D["cc2_out"], inc=[s_gp.inc(16)])
        act.wait(s_gp)
        act.activation(misc3[0:1, 10:11], smallrows[0:1, 13:14], AF.Tanh,
                       scale=0.5)
        act.mark(s_act)
        dve.wait(s_act)
        dve.tensor_scalar(misc3[0:1, 10:11], misc3[0:1, 10:11], 0.5, 0.5,
                          ALU.mult, ALU.add)
        dve.drain()
        dve.tensor_mul(misc3[0:1, 11:12], misc3[0:1, 10:11], misc3[0:1, 9:10])
        dve.drain()
        dve.tensor_scalar(misc3[0:1, 11:12], misc3[0:1, 11:12], 1.0 / QMAX, None,
                          ALU.mult)
        dve.drain()
        dve.reciprocal(misc3[0:1, 12:13], misc3[0:1, 11:12])
        dve.drain()
        dve.mark(s_dve)
        gp.wait(s_dve)
        gp.dma_start(out=D["sc_d"][0:1, 3:5], in_=misc3[0:1, 11:13],
                     inc=[s_gp.inc(16)])
        sp.wait(s_gp)
        sp.dma_start(out=sd_c[:, 0:1],
                     in_=D["sc_d"][0:1, 4:5].broadcast_to([128, 1]),
                     inc=[s_ld.inc(16)])
        sp.dma_start(out=sd_c[:, 1:2],
                     in_=D["sc_d"][0:1, 3:4].broadcast_to([128, 1]),
                     inc=[s_ld.inc(16)])
        act.wait(s_ld)
        act.mark(s_act)

        barrier()
        es_f.close()

    if lim >= 9:
        es_f2 = ExitStack()
        kOnesRf2 = mk(es_f2, "kOnesRf2", [1, 128], F32)
        f_wdT = mk(es_f2, "f_wdT", [128, CHI * HS], BF16)
        f_hf = mk(es_f2, "f_hf", [128, CHI * T512], F32)
        f_hi = mk(es_f2, "f_hi", [128, CHI * T512], I32)
        f_hb = mk(es_f2, "f_hb", [128, CHI * T512], BF16)
        f2_x = mk(es_f2, "f2_x", [128, HS], F32)
        f2_a = mk(es_f2, "f2_a", [128, HS], F32)
        f2_o = mk(es_f2, "f2_o", [128, HS], F32)

        for ic in range(CHI):
            sp.dma_start_transpose(f_wdT[:, ic * HS:(ic + 1) * HS],
                                   D["wd_bf"][:, ic * 128:(ic + 1) * 128],
                                   inc=[s_ld.inc(16)])
        n_wd = s_ld.n

        for tile512 in range(TOK // T512):
            tok0 = tile512 * T512
            sp.wait(s_pe)
            for ic in range(CHI):
                sp.dma_start(out=f_hf[:, ic * T512:(ic + 1) * T512],
                             in_=D["hpre_d"][ic * 128:(ic + 1) * 128,
                                             tok0:tok0 + T512],
                             inc=[s_ld.inc(16)])
            act.wait(s_ld)
            act.activation(f_hi, f_hf, AF.Identity, scale=sd_c[:, 0:1])
            act.drain()
            act.copy(f_hb, f_hi)
            act.mark(s_act)
            dve.wait(s_act)
            dve.tensor_scalar(f_hb, f_hb, QMAX, -QMAX - 1.0, ALU.min, ALU.max)
            dve.mark(s_dve)
            n_hq = s_dve.n
            pe.wait(s_dve, n_hq)
            pe.wait(s_ld, n_wd)
            ffb_act = s_act.n
            pe.wait(s_act)
            for ts in range(4):
                for nh2 in range(2):
                    bank = PB[(ts * 2 + nh2) % 2]
                    pe.wait(s_act, max(ffb_act, s_act.n - 1))
                    for ic in range(CHI):
                        pe.matmul(bank[:, 0:T512],
                                  f_hb[:, ic * T512 + ts * 128:
                                       ic * T512 + (ts + 1) * 128],
                                  f_wdT[:, ic * HS + nh2 * T512:
                                        ic * HS + (nh2 + 1) * T512],
                                  start=(ic == 0), stop=(ic == CHI - 1))
                    pe.mark(s_pe)
                    act.wait(s_pe)
                    act.wait(s_st, s_st.n)
                    act.activation(f2_o[:, nh2 * T512:(nh2 + 1) * T512],
                                   bank[:, 0:T512], AF.Identity,
                                   scale=sd_c[:, 1:2])
                    act.mark(s_act)
                t = tile512 * 4 + ts
                sp.wait(s_dve)
                sp.dma_start(out=f2_x, in_=X[t * 128:(t + 1) * 128, :],
                             inc=[s_ld.inc(16)])
                sp.dma_start(out=f2_a, in_=D["attn_d"][t * 128:(t + 1) * 128, :],
                             inc=[s_ld.inc(16)])
                dve.wait(s_ld)
                dve.wait(s_act)
                dve.tensor_add(f2_o, f2_o, f2_a)
                dve.drain()
                dve.tensor_add(f2_o, f2_o, f2_x)
                dve.mark(s_dve)
                sp.wait(s_dve)
                sp.dma_start(out=OUT[t * 128:(t + 1) * 128, :], in_=f2_o,
                             inc=[s_st.inc(16)])

        sp.wait(s_st, s_st.n)
        gp.wait(s_st, s_st.n)

    if dbg:
        barrier()
        sp.dma_start(out=DBG["mr"][:, 0:32], in_=M_sb, allow_slow_non_contiguous=True, inc=[s_ld.inc(16)])
        sp.dma_start(out=DBG["mr"][:, 32:64], in_=R_sb, allow_slow_non_contiguous=True, inc=[s_ld.inc(16)])
        sp.dma_start(out=DBG["mr"][:, 64:65], in_=xmax_sb, allow_slow_non_contiguous=True, inc=[s_ld.inc(16)])
        sp.dma_start(out=DBG["mr"][:, 65:66], in_=hmax_sb, allow_slow_non_contiguous=True, inc=[s_ld.inc(16)])
        sp.dma_start(out=DBG["misc"], in_=misc3, inc=[s_ld.inc(16)])
    sp.wait(s_ld, s_ld.n)
    sp.wait(s_gp, s_gp.n)
    sp.wait(s_st, s_st.n)
    sp.wait(s_cc, s_cc.n)

    with nc.Block() as block:
        em.flush(block)
    return nc, ()


def _shard_inputs(inputs):
    x = np.ascontiguousarray(
        np.asarray(inputs["L0_attn_hidden_states"], np.float32))
    consts = _host_constants()
    wmap = dict(
        Wq="L0_attn_Wq", Wk="L0_attn_Wk", Wv="L0_attn_Wv", Wb="L0_attn_Wb",
        Wo="L0_attn_Wo", W1="L0_attn_gate_W1", W2="L0_attn_gate_W2",
        convq="L0_attn_conv_q_w", convk="L0_attn_conv_k_w",
        convv="L0_attn_conv_v_w", b1="L0_attn_gate_b1", b2="L0_attn_gate_b2",
        cb="L0_attn_gate_copy_bias", lt="L0_attn_gate_log_temp",
        onw="L0_attn_o_norm_w", clipx="L0_ffn_clip_x",
        clipd="L0_ffn_clip_down",
        wg="L0_ffn_w_gate", wu="L0_ffn_w_up", wd="L0_ffn_w_down",
    )
    base = {k: np.ascontiguousarray(np.asarray(inputs[v], np.float32))
            for k, v in wmap.items()}
    base["firs"] = np.ascontiguousarray(
        np.asarray(inputs["L0_attn_fir_short_filt"],
                   np.float32).reshape(NH * DV, FIRS))
    base["firl"] = np.ascontiguousarray(
        np.asarray(inputs["L0_attn_fir_long_filt"],
                   np.float32).reshape(NH * DV, FIRL))
    base.update(consts)
    in_maps = []
    for c in range(NCORES):
        d = dict(base)
        d["x"] = np.ascontiguousarray(
            x[c * SEQ_PER_CORE:(c + 1) * SEQ_PER_CORE].reshape(TOK, HS))
        in_maps.append(d)
    return in_maps


def kernel(**inputs):
    nc, _ = build()
    in_maps = _shard_inputs(inputs)
    res = run_bass_kernel_spmd(nc, in_maps, core_ids=list(range(NCORES)))
    out = np.concatenate(
        [r["out"].reshape(SEQ_PER_CORE, L, HS) for r in res.results], axis=0)
    return out.astype(np.float32)
